# version 12 (final)
# speedup vs baseline: 1.5154x; 1.0046x over previous
import os as _os
import numpy as np
import ml_dtypes
from contextlib import ExitStack
import concourse.bass as bass
import concourse.mybir as mybir
from concourse.bass_utils import run_bass_kernel_spmd

F32 = mybir.dt.float32
BF16 = mybir.dt.bfloat16
U8 = mybir.dt.uint8
I32 = mybir.dt.int32
AF = mybir.ActivationFunctionType
ALU = mybir.AluOpType
AX = mybir.AxisListType
bfloat16 = ml_dtypes.bfloat16

P = 128
T = 2048
D = 2048
DC = D // P
NCORES = 8
LN_EPS = 1e-5
SEM_ROLL = 30000


def tile_k_major(w, cpl):
    K, M = w.shape
    nk = K // P
    ns = M // cpl
    r = w.reshape(nk, P, ns, cpl)
    return np.ascontiguousarray(r.transpose(2, 1, 0, 3).reshape(ns, P, nk * cpl))


def host_consts():
    c = {}
    c["ones_bf"] = np.ones((P, P), dtype=np.float32).astype(bfloat16)
    c["ident_bf"] = np.eye(P, dtype=np.float32).astype(bfloat16)
    ROT, BASE = 32, 10000.0
    inv = 1.0 / BASE ** (np.arange(0, ROT, 2, dtype=np.float64) / ROT)
    fr = np.outer(inv, np.arange(T, dtype=np.float64))
    cos, sin = np.cos(fr), np.sin(fr)
    C = np.ones((P, T), dtype=np.float64)
    S = np.zeros((P, T), dtype=np.float64)
    for off in (0, 64):
        C[off:off + 16] = cos
        C[off + 16:off + 32] = cos
        S[off:off + 16] = -sin
        S[off + 16:off + 32] = sin
    c["ropeC_bf"] = C.astype(np.float32).astype(bfloat16)
    c["ropeS_bf"] = S.astype(np.float32).astype(bfloat16)
    kp = np.arange(P)[:, None]
    qf = np.arange(512)[None, :]
    dm = np.concatenate([(128 * j + kp <= qf).astype(np.float32) for j in range(4)], axis=1)
    c["dmask_bf"] = dm.astype(bfloat16)
    m16 = np.zeros((16, P), dtype=np.float32)
    m16[np.arange(P) % 16, np.arange(P)] = 1.0
    c["map16_f"] = m16
    c["map16T_f"] = np.ascontiguousarray(m16.T)
    return c


def prep_l0(inp):
    w = {}
    wqkv = np.asarray(inp["L0_attn_Wqkv"], dtype=np.float32)
    w["l0_wqkv"] = tile_k_major(wqkv, P).astype(bfloat16)
    bqkv = np.asarray(inp["L0_attn_bqkv"], dtype=np.float32)
    w["l0_bqk"] = np.ascontiguousarray(bqkv[:4096].reshape(32, P).T)
    w["l0_bv"] = np.ascontiguousarray(bqkv[4096:].reshape(1, 2048))
    wout = np.asarray(inp["L0_attn_Wout"], dtype=np.float32)
    w["l0_wout"] = np.ascontiguousarray(
        wout.reshape(8, 2, P, DC, P).transpose(0, 2, 3, 1, 4).reshape(8, P, DC * 2 * P)).astype(bfloat16)
    w["l0_bout"] = np.ascontiguousarray(
        np.asarray(inp["L0_attn_bout"], dtype=np.float32).reshape(DC, P).T)
    w["l0_rw"] = tile_k_major(np.asarray(inp["L0_ffn_router_w"], dtype=np.float32), 16)[0]
    w["l0_rb"] = np.asarray(inp["L0_ffn_router_b"], dtype=np.float32).reshape(1, 16).copy()
    w["l0_we"] = tile_k_major(np.asarray(inp["L0_ffn_expert_w"], dtype=np.float32), P).astype(bfloat16)
    w["l0_be"] = np.ascontiguousarray(
        np.asarray(inp["L0_ffn_expert_b"], dtype=np.float32).reshape(DC, P).T)
    return w



class Seq:

    ENGS = ("tensor", "vector", "scalar", "gpsimd", "sync")

    def __init__(self, nc, ctx, block):
        self.nc, self.ctx, self.block = nc, ctx, block
        self.sems = {}
        self.bases = {}
        self.cnt = {}
        self.vis = {}
        self.rr = {}
        self.seen = {e: {} for e in self.ENGS}
        for e in self.ENGS:
            self._ensure(e)

    def _ensure(self, name):
        if name not in self.sems:
            self.sems[name] = [self.ctx.enter_context(
                self.nc.semaphore(f"q_{name}_0"))]
            self.bases[name] = [0]
            self.cnt[name] = 0
            self.vis[name] = 0
            for e in self.ENGS:
                self.seen[e].setdefault(name, 0)

    def _cur_sem(self, name):
        if self.cnt[name] - self.bases[name][-1] > SEM_ROLL:
            self.sems[name].append(self.ctx.enter_context(
                self.nc.semaphore(f"q_{name}_{len(self.sems[name])}")))
            self.bases[name].append(self.cnt[name])
        return self.sems[name][-1]

    def mark(self):
        return dict(self.vis)

    @staticmethod
    def merge(*ms):
        out = {}
        for m in ms:
            if not m:
                continue
            for k, v in m.items():
                if v > out.get(k, 0):
                    out[k] = v
        return out

    SELF_EDGE = set(_os.environ.get("SEQ_SELF_EDGE", "vector,scalar,gpsimd").split(","))

    def _emit_waits(self, eng, engine_name, target):
        if not target:
            return
        seen = self.seen[engine_name]
        for nm, v in target.items():
            if nm == engine_name:
                if nm not in self.SELF_EDGE:
                    continue
                v = min(v, self.cnt[nm])
            if v <= 0:
                continue
            if seen.get(nm, 0) >= v:
                continue
            i = len(self.bases[nm]) - 1
            while self.bases[nm][i] >= v:
                i -= 1
            eng.wait_ge(self.sems[nm][i], v - self.bases[nm][i])
            seen[nm] = v

    def step(self, engine, fn, wait=None, inc=1):
        target = self.mark() if wait is None else wait
        sem = self._cur_sem(engine)
        myinc = inc

        def body(eng):
            self._emit_waits(eng, engine, target)
            last = fn(eng)
            if last is not None:
                last.then_inc(sem, myinc)

        getattr(self.block, engine)(body)
        self.cnt[engine] += myinc
        self.vis[engine] = self.cnt[engine]

    def dma(self, engine, transfers, wait=None, stream=None, ghost=False):
        if stream is None:
            k = self.rr.get(engine, 0)
            self.rr[engine] = (k + 1) % 4
            stream = f"dma_{engine}_{k}"
            self._ensure(stream)
            target = Seq.merge(self.mark() if wait is None else wait,
                               {stream: self.cnt[stream]})
        else:
            self._ensure(stream)
            target = self.mark() if wait is None else wait
        sem = self._cur_sem(stream)

        def body(eng):
            self._emit_waits(eng, engine, target)
            for o, i in transfers:
                eng.dma_start(out=o, in_=i).then_inc(sem, 16)

        getattr(self.block, engine)(body)
        self.cnt[stream] += 16 * len(transfers)
        if not ghost:
            self.vis[stream] = self.cnt[stream]
        return {stream: self.cnt[stream]}


class BK:
    def __init__(self, nc, ctx, block):
        self.nc, self.ctx, self.block = nc, ctx, block
        self.seq = Seq(nc, ctx, block)
        self._alloc_ctx = [ctx]

    def sb(self, name, shape, dt=BF16):
        return self._alloc_ctx[-1].enter_context(self.nc.sbuf_tensor(name, shape, dt))

    def ps(self, name, shape, dt=F32):
        return self._alloc_ctx[-1].enter_context(self.nc.psum_tensor(name, shape, dt))

    def push_scope(self):
        es = ExitStack()
        self._alloc_ctx.append(es)
        return es

    def pop_scope(self):
        self._alloc_ctx.pop().close()



def emit_ln(bk, cm):
    s = bk.seq
    x_fm, xn_fm = cm["x_fm"], cm["xn_fm"]
    psA, psB = cm["psA"], cm["psB"]
    ones = cm["ones"]
    sq = cm["sq8k"]
    f_a, f_b = cm["f2k_a"], cm["f2k_b"]

    def sums(t):
        r = None
        for tt in range(4):
            for kd in range(DC):
                r = t.matmul(psA[:, tt * 512:(tt + 1) * 512], ones[:],
                             x_fm[:, kd * T + tt * 512: kd * T + (tt + 1) * 512],
                             start=(kd == 0), stop=(kd == DC - 1))
        return r
    s.step("tensor", sums)

    for tt in range(4):
        for kg in range(4):
            def sqs(v, tt=tt, kg=kg):
                r = None
                for j in range(4):
                    kd = kg * 4 + j
                    xs = x_fm[:, kd * T + tt * 512: kd * T + (tt + 1) * 512]
                    r = v.tensor_mul(sq[:, j * 512:(j + 1) * 512], xs, xs)
                return r
            s.step("vector", sqs)

            def ssq(t, tt=tt, kg=kg):
                r = None
                for j in range(4):
                    r = t.matmul(psB[:, tt * 512:(tt + 1) * 512], ones[:],
                                 sq[:, j * 512:(j + 1) * 512],
                                 start=(kg == 0 and j == 0), stop=(kg == 3 and j == 3))
                return r
            s.step("tensor", ssq)

    s.step("vector", lambda v: v.tensor_scalar_mul(f_a[:], psA[:], 1.0 / D))
    s.step("scalar", lambda sc: sc.activation(sq[:], f_a[:], AF.Square))
    s.step("vector", lambda v: v.scalar_tensor_tensor(
        f_b[:], psB[:], 1.0 / D, sq[:], op0=ALU.mult, op1=ALU.subtract))
    s.step("scalar", lambda sc: sc.activation(sq[:], f_b[:], AF.Sqrt, bias=cm["epsc"][:, 0:1]))
    s.step("vector", lambda v: v.reciprocal(f_b[:], sq[:]))
    s.step("vector", lambda v: v.tensor_mul(sq[:], f_a[:], f_b[:]))
    for kd in range(DC):
        def xn_mul(v, kd=kd):
            xs = x_fm[:, kd * T:(kd + 1) * T]
            xo = xn_fm[:, kd * T:(kd + 1) * T]
            return v.tensor_mul(xo, xs, f_b[:])
        s.step("vector", xn_mul)

        def xn_sub(v, kd=kd):
            xo = xn_fm[:, kd * T:(kd + 1) * T]
            return v.tensor_sub(xo, xo, sq[:])
        s.step("vector", xn_sub)



def emit_block0(bk, cm, Wap):
    nc = bk.nc
    s = bk.seq
    x_fm, xn_fm = cm["x_fm"], cm["xn_fm"]
    psA, psB = cm["psA"], cm["psB"]
    ones, ident = cm["ones"], cm["ident"]
    identf = cm["ident_f32"]
    onesf = cm["onesf_row"]
    scr2k = cm["sq8k"]
    f_a, f_b = cm["f2k_a"], cm["f2k_b"]

    ropeC = bk.sb("ropeC", [P, T], BF16)
    ropeS = bk.sb("ropeS", [P, T], BF16)
    dmask = bk.sb("dmask", [P, 4 * 512], BF16)
    map16 = bk.sb("map16", [16, P], F32)
    map16T = bk.sb("map16T", [P, 16], F32)
    wsl = [bk.sb(f"b0wsl{i}", [P, DC * P], BF16) for i in range(3)]
    wout_sl = bk.sb("b0wout", [P, DC * 2 * P], BF16)
    wexp_sl = [bk.sb(f"b0wexp{i}", [P, DC * P], BF16) for i in range(2)]
    rw_sl = bk.sb("b0rw", [P, DC * 16], BF16)
    rb_sb = bk.sb("b0rb", [1, 16], F32)
    rbT = bk.sb("b0rbT", [16, 1], F32)
    bqk_sb = bk.sb("b0bqk", [P, 32], F32)
    bv_sb = bk.sb("b0bv", [1, P], BF16)
    bout_sb = bk.sb("b0bout", [P, DC], F32)
    be_sb = bk.sb("b0be", [P, DC], F32)
    qt_pair = bk.sb("b0q", [P, T], BF16)
    kt_pair = bk.sb("b0k", [P, T], BF16)
    v_aug = bk.sb("b0v", [P, 16 * 130], BF16)
    p_big = bk.sb("b0pbig", [P, 4 * 512], BF16)
    nrm = bk.sb("b0nrm", [P, 4], F32)
    recips = scr2k[:].bitcast(F32)[:, 128:132]
    out_tok = scr2k[:, 0:256]
    attnT = bk.sb("b0attnT", [P, 2 * T], BF16)
    attnT_f = attnT[:].bitcast(F32)
    s_tok = p_big[:].bitcast(F32)[:, 0:256]
    lo16 = bk.sb("b0lo", [16, 1], F32)
    hi16 = bk.sb("b0hi", [16, 1], F32)
    mid16 = bk.sb("b0mid", [16, 1], F32)
    cnt16 = bk.sb("b0cnt", [16, 1], F32)
    gemask = bk.sb("b0gem", [16, 1], F32)
    tmp16 = bk.sb("b0tmp", [16, 1], F32)
    thr_col = bk.sb("b0thr", [P, 1], F32)
    cnt_col = bk.sb("b0ccol", [P, 1], F32)
    rsel2 = qt_pair[:].bitcast(F32)[:, 0:128]
    smax16 = bk.sb("b0smax", [P, 16], F32)

    agi = nc.dram_tensor("l0_agi", [16, T], F32, kind="Internal")
    ago = nc.dram_tensor("l0_ago", [P, T], F32, kind="Internal")

    s.dma("gpsimd", [
        (ropeC[:], Wap["ropeC_bf"].ap()),
        (ropeS[:], Wap["ropeS_bf"].ap()),
        (dmask[:], Wap["dmask_bf"].ap()),
        (map16[:], Wap["map16_f"].ap()),
        (map16T[:], Wap["map16T_f"].ap()),
        (rw_sl[:], Wap["l0_rw"].ap()),
        (rb_sb[:], Wap["l0_rb"].ap()),
        (bqk_sb[:], Wap["l0_bqk"].ap()),
        (bout_sb[:], Wap["l0_bout"].ap()),
        (be_sb[:], Wap["l0_be"].ap()),
    ], wait={})

    emit_ln(bk, cm)

    sT_sb = attnT_f[0:16, 0:T]
    s.step("tensor", lambda t: t.matmul(psB[0:16, 0:1], rb_sb[:], onesf[0:1, 0:1],
                                        start=True, stop=True))
    s.step("vector", lambda v: v.tensor_copy(rbT[:], psB[0:16, 0:1]))

    def rmm(t):
        r = None
        for tt in range(4):
            for kd in range(DC):
                r = t.matmul(psA[0:16, tt * 512:(tt + 1) * 512],
                             rw_sl[:, kd * 16:(kd + 1) * 16],
                             xn_fm[:, kd * T + tt * 512: kd * T + (tt + 1) * 512],
                             start=(kd == 0), stop=(kd == DC - 1))
        return r
    s.step("tensor", rmm)
    s.step("scalar", lambda sc: sc.activation(sT_sb, psA[0:16, :],
                                              AF.Identity, bias=rbT[:]))

    def ltr(t):
        r = None
        for i in range(16):
            r = t.transpose(psB[:, i * 16:(i + 1) * 16],
                            sT_sb[0:16, i * P:(i + 1) * P], identf[0:16, 0:16])
        return r
    s.step("tensor", ltr)
    pview = psB[:, 0:256].rearrange("p (i e) -> p i e", e=16)
    s.step("vector", lambda v: v.reduce_max(smax16[:], pview, axis=AX.X))

    def subexp(v):
        r = None
        for i in range(16):
            r = v.tensor_scalar(s_tok[:, i * 16:(i + 1) * 16], psB[:, i * 16:(i + 1) * 16],
                                smax16[:, i:i + 1], None, op0=ALU.subtract)
        return r
    s.step("vector", subexp)
    s.step("scalar", lambda sc: sc.activation(s_tok, s_tok, AF.Exp))
    s.step("vector", lambda v: v.reduce_sum(
        smax16[:], s_tok.rearrange("p (i e) -> p i e", e=16), axis=AX.X))
    s.step("vector", lambda v: v.reciprocal(smax16[:], smax16[:]))

    def normmul(v):
        r = None
        for i in range(16):
            r = v.tensor_scalar(s_tok[:, i * 16:(i + 1) * 16], s_tok[:, i * 16:(i + 1) * 16],
                                smax16[:, i:i + 1], None, op0=ALU.mult)
        return r
    s.step("vector", normmul)

    def str_(t):
        r = None
        for i in range(16):
            r = t.transpose(psA[0:16, i * P:(i + 1) * P],
                            s_tok[:, i * 16:(i + 1) * 16], identf[:])
        return r
    s.step("tensor", str_)
    s.step("vector", lambda v: v.tensor_copy(sT_sb, psA[0:16, 0:T]))
    s.dma("gpsimd", [(agi.ap(), sT_sb)])
    s.step("gpsimd", lambda g: g.collective_compute(
        "AllGather", ALU.bypass, replica_groups=[list(range(NCORES))],
        ins=[agi.ap().opt()], outs=[ago.ap().opt()]))
    ag_done = s.mark()

    SKIP_ATTN = _os.environ.get("B0_SKIP_ATTN", "") == "1"
    SKIP_FFN = _os.environ.get("B0_SKIP_FFN", "") == "1"

    PERM = [(i + 16) % 32 for i in range(32)]
    td0 = lambda: {"tensor": s.cnt["tensor"]}
    vd0 = lambda: {"vector": s.cnt["vector"]}
    sd0 = lambda: {"scalar": s.cnt["scalar"]}

    NPAIRS = int(_os.environ.get("B0_PAIRS", "16"))
    base0 = s.mark()
    qkv_ld = {}
    norm_last = [None]
    av_last = None

    def load_pair_w(hp, wait):
        return s.dma("gpsimd", [
            (wsl[0][:], Wap["l0_wqkv"].ap()[hp]),
            (wsl[1][:], Wap["l0_wqkv"].ap()[16 + hp]),
            (wsl[2][:], Wap["l0_wqkv"].ap()[32 + hp]),
            (bv_sb[:], Wap["l0_bv"].ap()[0:1, hp * P:(hp + 1) * P]),
        ], stream="b0qkv", ghost=True, wait=wait)

    if not SKIP_ATTN:
        qkv_ld[0] = load_pair_w(0, base0)
    wfree_psA = base0
    wfree_psB = base0
    wfree_o = base0
    for hp in (range(NPAIRS) if not SKIP_ATTN else []):
        wq, wk, wv = wsl[0], wsl[1], wsl[2]

        rope_done = {}
        ev_prev = None
        for which, wmat, dst in (("q", wq, qt_pair), ("k", wk, kt_pair)):
            def proj(t, wmat=wmat):
                r = None
                for tt in range(4):
                    for kd in range(DC):
                        r = t.matmul(psA[:, tt * 512:(tt + 1) * 512],
                                     wmat[:, kd * P:(kd + 1) * P],
                                     xn_fm[:, kd * T + tt * 512: kd * T + (tt + 1) * 512],
                                     start=(kd == 0), stop=(kd == DC - 1))
                return r
            s.step("tensor", proj, wait=Seq.merge(
                qkv_ld[hp], wfree_psA if which == "q" else ev_prev))
            proj_d = td0()
            bcol = bqk_sb[:, (0 if which == "q" else 16) + hp:
                          (0 if which == "q" else 16) + hp + 1]
            s.step("scalar", lambda sc, bcol=bcol, dst=dst: sc.activation(
                dst[:], psA[:], AF.Identity, bias=bcol), wait=proj_d)
            ev_d = sd0()
            ev_prev = ev_d

            def rope_sh(v, dst=dst):
                r = None
                for tt in range(4):
                    r = v.stream_shuffle(scr2k[:, tt * 512:(tt + 1) * 512],
                                         dst[:, tt * 512:(tt + 1) * 512], PERM)
                return r
            s.step("vector", rope_sh, wait=ev_d)
            s.step("vector", lambda v, dst=dst: v.tensor_mul(dst[:], dst[:], ropeC[:]),
                   wait=vd0())
            s.step("vector", lambda v: v.tensor_mul(scr2k[:], scr2k[:], ropeS[:]),
                   wait=vd0())
            s.step("vector", lambda v, dst=dst: v.tensor_add(dst[:], dst[:], scr2k[:]),
                   wait=vd0())
            rope_done[which] = vd0()

        s.step("vector", lambda v: v.memset(
            v_aug[:].rearrange("p (i c) -> p i c", c=130)[:, :, 64:65], 1.0),
            wait=base0 if hp == 0 else av_last)
        s.step("vector", lambda v: v.memset(
            v_aug[:].rearrange("p (i c) -> p i c", c=130)[:, :, 129:130], 1.0),
            wait=vd0())
        vev_d = {}
        for grp in range(4):
            def vproj(t, grp=grp, wv=wv):
                r = None
                for j in range(4):
                    ti = grp * 4 + j
                    dstp = psB[:, j * 512: j * 512 + P]
                    for kd in range(DC):
                        t.matmul(dstp, xn_fm[:, kd * T + ti * P: kd * T + (ti + 1) * P],
                                 wv[:, kd * P:(kd + 1) * P],
                                 start=(kd == 0), stop=False)
                    r = t.matmul(dstp, ones[0:1, :], bv_sb[:],
                                 start=False, stop=True)
                return r
            s.step("tensor", vproj, wait=Seq.merge(
                wfree_psB if grp == 0 else None, vev_d.get(grp - 1)))
            vproj_d = td0()

            def vevac(sc, grp=grp):
                dst = v_aug[:].rearrange("p (i h c) -> p i h c", h=2, c=65)[
                    :, grp * 4:(grp + 1) * 4, :, 0:64]
                src = psB[:].rearrange("p (i c) -> p i c", c=512)[:, :, 0:128]
                src = src.rearrange("p i (h c) -> p i h c", h=2)
                return sc.activation(dst, src, AF.Copy)
            s.step("scalar", vevac, wait=vproj_d)
            vev_d[grp] = sd0()
        if hp + 1 < NPAIRS:
            qkv_ld[hp + 1] = load_pair_w(hp + 1, td0())

        s.step("vector", lambda v: v.tensor_mul(scr2k[:], qt_pair[:], qt_pair[:]),
               wait=rope_done["q"])
        sqq_d = vd0()

        def nmmq(t):
            r = None
            for tt in range(4):
                r = t.matmul(psA[:, tt * 512:(tt + 1) * 512], ones[:],
                             scr2k[:, tt * 512:(tt + 1) * 512], start=True, stop=True)
            return r
        s.step("tensor", nmmq, wait=sqq_d)
        nmmq_d = td0()
        s.step("vector", lambda v: v.tensor_mul(scr2k[:], kt_pair[:], kt_pair[:]),
               wait=Seq.merge(nmmq_d, rope_done["k"]))
        sqk_d = vd0()

        def nmmk(t):
            r = None
            for tt in range(4):
                r = t.matmul(psB[:, tt * 512:(tt + 1) * 512], ones[:],
                             scr2k[:, tt * 512:(tt + 1) * 512], start=True, stop=True)
            return r
        s.step("tensor", nmmk, wait=Seq.merge(sqk_d, vev_d[3]))
        nmmk_d = td0()

        def nred(v):
            v.reduce_max(nrm[:, 0:1], psA[:], axis=AX.X)
            return v.reduce_max(nrm[:, 1:2], psB[:], axis=AX.X)
        s.step("vector", nred, wait=nmmk_d)
        s.step("vector", lambda v: v.tensor_mul(nrm[:, 2:3], nrm[:, 0:1], nrm[:, 1:2]),
               wait=vd0())
        s.step("scalar", lambda sc: sc.activation(nrm[:, 3:4], nrm[:, 2:3],
                                                  AF.Sqrt, scale=1.0 / 64.0), wait=vd0())
        s.step("vector", lambda v: v.tensor_scalar_mul(nrm[:, 0:1], nrm[:, 3:4], -1.0),
               wait=sd0())
        negb_d = vd0()

        exp_d = {}
        av_d = {}
        trev_prev = None
        gg = [0]
        for h2 in range(2):
            hb = h2 * 64
            for qt in range(4):
                q0 = qt * 512
                L = 2 * (qt + 1)
                g0 = gg[0]

                def sco(t, ih, hb=hb, q0=q0, g0=g0):
                    par = ih % 2
                    r = None
                    for j2 in range(2):
                        kt = ih * 2 + j2
                        r = t.matmul(psA[:, par * 1024 + j2 * 512:
                                         par * 1024 + (j2 + 1) * 512],
                                     kt_pair[hb:hb + 64, kt * P:(kt + 1) * P],
                                     qt_pair[hb:hb + 64, q0:q0 + 512],
                                     start=True, stop=True)
                    return r

                def av(t, ih, h2=h2, qt=qt, L=L):
                    par = ih % 2
                    r = None
                    for j2 in range(2):
                        kt = ih * 2 + j2
                        for qs in range(4):
                            r = t.matmul(
                                psB[:, qs * 512: qs * 512 + 65],
                                p_big[:, par * 1024 + j2 * 512 + qs * 128:
                                      par * 1024 + j2 * 512 + (qs + 1) * 128],
                                v_aug[:, kt * 130 + h2 * 65: kt * 130 + h2 * 65 + 65],
                                start=(ih == 0 and j2 == 0),
                                stop=(ih == L - 1 and j2 == 1))
                    return r

                def emit_av(ihh, g0=g0, avfn=av):
                    s.step("tensor", lambda t, ihh=ihh: avfn(t, ihh),
                           wait=Seq.merge(exp_d[g0 + ihh],
                                          norm_last[0] if ihh == 0 else None))
                    av_d[g0 + ihh] = td0()

                for ih in range(L):
                    g = g0 + ih
                    w_sco = Seq.merge(
                        exp_d.get(g - 2) if g >= 2 else negb_d,
                        trev_prev if ih == 1 else None,
                        negb_d if g == 0 else None)
                    s.step("tensor", lambda t, ih=ih: sco(t, ih), wait=w_sco)
                    sco_d = td0()
                    s.step("scalar", lambda sc, par=ih % 2: sc.activation(
                        p_big[:, par * 1024: par * 1024 + 1024],
                        psA[:, par * 1024: par * 1024 + 1024],
                        AF.Exp, bias=nrm[:, 0:1], scale=0.125),
                        wait=Seq.merge(sco_d, av_d.get(g - 2)))
                    exp_d[g] = sd0()
                    if ih >= L - 2:
                        def dmul(v, ih=ih, qt=qt):
                            par = ih % 2
                            r = None
                            for j2 in range(2):
                                jj = (ih * 2 + j2) - 4 * qt
                                r = v.tensor_mul(
                                    p_big[:, par * 1024 + j2 * 512:
                                          par * 1024 + (j2 + 1) * 512],
                                    p_big[:, par * 1024 + j2 * 512:
                                          par * 1024 + (j2 + 1) * 512],
                                    dmask[:, jj * 512:(jj + 1) * 512])
                            return r
                        s.step("vector", dmul, wait=exp_d[g])
                        exp_d[g] = Seq.merge(exp_d[g], vd0())
                    if _os.environ.get("B0X_NOSHIFT", "") == "1":
                        emit_av(ih)
                    elif ih >= 1:
                        emit_av(ih - 1)
                if _os.environ.get("B0X_NOSHIFT", "") != "1":
                    emit_av(L - 1)
                gg[0] = g0 + L

                if (_os.environ.get("B0X_D2", "") == "1" and hp == 0 and h2 == 0
                        and qt == 0):
                    s.step("vector", lambda v: v.tensor_copy(x_fm[:, 0:T], psB[:]),
                           wait=av_d[gg[0] - 1])
                    s.step("vector", lambda v: v.tensor_copy(x_fm[:, T:2 * T], p_big[:]),
                           wait={"vector": s.cnt["vector"]})
                s.step("vector", lambda v: v.reciprocal(
                    recips, psB[:].rearrange("p (q c) -> p q c", c=512)[:, :, 64:65]),
                    wait=av_d[gg[0] - 1])

                def norm(v):
                    r = None
                    for qs in range(4):
                        r = v.tensor_scalar(out_tok[:, qs * 64:(qs + 1) * 64],
                                            psB[:, qs * 512: qs * 512 + 64],
                                            recips[:, qs:qs + 1], None, op0=ALU.mult)
                    return r
                s.step("vector", norm, wait=vd0())
                norm_last[0] = vd0()

                def tr(t):
                    r = None
                    ptr = psA[:, 1536:2048].bitcast(BF16)
                    for qs in range(4):
                        r = t.transpose(ptr[0:64, qs * 128:(qs + 1) * 128],
                                        out_tok[:, qs * 64:(qs + 1) * 64], ident[:])
                    return r
                s.step("tensor", tr, wait=norm_last[0])
                tr_d = td0()

                def trev(v, hp=hp, h2=h2, qt=qt):
                    ptr = psA[:, 1536:2048].bitcast(BF16)
                    dst = attnT[h2 * 64: h2 * 64 + 64,
                                (hp % 2) * T + qt * 512: (hp % 2) * T + (qt + 1) * 512]
                    return v.tensor_copy(dst, ptr[0:64, 0:512])
                s.step("vector", trev, wait=tr_d)
                trev_prev = vd0()
        av_last = av_d[gg[0] - 1]
        wfree_psA = Seq.merge(exp_d[gg[0] - 1], trev_prev)
        wfree_psB = trev_prev

        if hp % 2 == 1:
            g2 = hp // 2
            wo_ld = s.dma("gpsimd", [(wout_sl[:], Wap["l0_wout"].ap()[g2])],
                          stream="b0wo", ghost=True, wait=wfree_o)
            acc_d = {}
            omm_d = {}
            for Dc in range(DC):
                ps = psA if Dc % 2 == 0 else psB
                def omm(t, Dc=Dc, ps=ps):
                    r = None
                    for tt in range(4):
                        for fc in range(2):
                            r = t.matmul(ps[:, tt * 512:(tt + 1) * 512],
                                         wout_sl[:, (Dc * 2 + fc) * P:(Dc * 2 + fc + 1) * P],
                                         attnT[:, fc * T + tt * 512: fc * T + (tt + 1) * 512],
                                         start=(fc == 0), stop=(fc == 1))
                    return r
                s.step("tensor", omm, wait=Seq.merge(
                    wo_ld, wfree_psA if Dc == 0 else None,
                    wfree_psB if Dc == 1 else None, acc_d.get(Dc - 2)))
                omm_d[Dc] = td0()
                if g2 == 0:
                    s.step("scalar", lambda sc, Dc=Dc, ps=ps: sc.activation(
                        f_a[:], ps[:], AF.Identity, bias=bout_sb[:, Dc:Dc + 1]),
                        wait=Seq.merge(omm_d[Dc], acc_d.get(Dc - 1)))
                    bia_d = sd0()

                    def acc0(v, Dc=Dc):
                        xo = x_fm[:, Dc * T:(Dc + 1) * T]
                        return v.tensor_add(xo, xo, f_a[:])
                    s.step("vector", acc0, wait=bia_d)
                else:
                    def accg(v, Dc=Dc, ps=ps):
                        xo = x_fm[:, Dc * T:(Dc + 1) * T]
                        return v.tensor_add(xo, xo, ps[:])
                    s.step("vector", accg, wait=omm_d[Dc])
                acc_d[Dc] = vd0()
            wfree_o = omm_d[DC - 1]
            wfree_psA = acc_d[14]
            wfree_psB = acc_d[15]

    if _os.environ.get("B0X_DUMPA", "") == "1":
        s.step("vector", lambda v: v.tensor_copy(x_fm[:, 0:2 * T], attnT[:, 0:2 * T]))
    if _os.environ.get("B0_DUMP", "") == "g0":
        s.step("vector", lambda v: v.tensor_copy(xn_fm[:, 0:T], x_fm[:, 0:T]))
        s.step("vector", lambda v: v.tensor_copy(xn_fm[:, T:2 * T], attnT[:, 0:T]))

    sag_sb = attnT_f
    s.dma("gpsimd", [(sag_sb, ago.ap()), (rsel2, Wap["rank_sel"].ap())])
    DUMP_FS = _os.environ.get("B0_DUMP", "") == "fs"
    if DUMP_FS:
        s.step("vector", lambda v: v.tensor_copy(xn_fm[:, 0:T], x_fm[:, 0:T]))
    s.step("vector", lambda v: v.memset(lo16[:], 0.0))
    s.step("vector", lambda v: v.memset(hi16[:], 1.0))
    s.step("vector", lambda v: v.memset(mid16[:], 0.5))
    s.step("vector", lambda v: v.memset(thr_col[:], 0.5))
    dump_thr = _os.environ.get("B0_DUMP", "") == "thr"
    for it in range(26):
        s.step("vector", lambda v: v.tensor_scalar(
            p_big[:], sag_sb, thr_col[:], None, op0=ALU.is_ge))
        s.step("vector", lambda v: v.reduce_sum(cnt_col[:], p_big[:], axis=AX.X))
        s.step("tensor", lambda t: t.matmul(psB[0:16, 0:1], map16T[:], cnt_col[:],
                                            start=True, stop=True))
        s.step("vector", lambda v: v.tensor_copy(cnt16[:], psB[0:16, 0:1]))
        if dump_thr:
            s.step("vector", lambda v, it=it: v.tensor_copy(
                x_fm[0:16, 100 + it:101 + it], cnt16[:]))
            s.step("vector", lambda v, it=it: v.tensor_copy(
                x_fm[0:16, 130 + it:131 + it], gemask[:]))
            s.step("vector", lambda v, it=it: v.tensor_copy(
                x_fm[0:16, 160 + it:161 + it], mid16[:]))
            s.step("vector", lambda v, it=it: v.tensor_copy(
                x_fm[0:16, 190 + it:191 + it], lo16[:]))
        s.step("vector", lambda v: v.tensor_scalar(gemask[:], cnt16[:], 1024.0, None,
                                                   op0=ALU.is_ge))

        s.step("vector", lambda v: v.tensor_sub(tmp16[:], mid16[:], lo16[:]))
        s.step("vector", lambda v: v.tensor_mul(tmp16[:], tmp16[:], gemask[:]))
        s.step("vector", lambda v: v.tensor_add(lo16[:], lo16[:], tmp16[:]))
        s.step("vector", lambda v: v.tensor_sub(tmp16[:], hi16[:], mid16[:]))
        s.step("vector", lambda v: v.tensor_mul(tmp16[:], tmp16[:], gemask[:]))
        s.step("vector", lambda v: v.tensor_add(hi16[:], tmp16[:], mid16[:]))
        s.step("vector", lambda v: v.tensor_add(mid16[:], lo16[:], hi16[:]))
        s.step("vector", lambda v: v.tensor_scalar_mul(mid16[:], mid16[:], 0.5))
        s.step("tensor", lambda t: t.matmul(psB[:, 512:513], map16[:], mid16[:],
                                            start=True, stop=True))
        s.step("vector", lambda v: v.tensor_copy(thr_col[:], psB[:, 512:513]))
    if DUMP_FS:
        s.step("vector", lambda v: v.tensor_copy(xn_fm[:, T:2 * T], x_fm[:, 0:T]))
    s.step("tensor", lambda t: t.matmul(psB[:, 512:513], map16[:], lo16[:],
                                        start=True, stop=True))
    s.step("vector", lambda v: v.tensor_copy(thr_col[:], psB[:, 512:513]))
    s.step("vector", lambda v: v.tensor_scalar(p_big[:], sag_sb, thr_col[:], None,
                                               op0=ALU.is_ge))
    s.step("vector", lambda v: v.tensor_mul(sag_sb, p_big[:], sag_sb))

    def sbc(t):
        r = None
        for tt in range(4):
            r = t.matmul(psB[:, tt * 512:(tt + 1) * 512], rsel2,
                         sag_sb[:, tt * 512:(tt + 1) * 512], start=True, stop=True)
        return r
    s.step("tensor", sbc)
    s.step("vector", lambda v: v.tensor_copy(f_b[:], psB[:]))

    if DUMP_FS:
        s.step("vector", lambda v: v.tensor_copy(xn_fm[:, 2 * T:3 * T], x_fm[:, 0:T]))
    if not SKIP_FFN:
        ld_end = {}
        ev_end = {}
        emm_end = {}
        for j in range(2):
            s.dma("gpsimd", [(wexp_sl[j][:], Wap["l0_we"].ap()[j])], stream=f"l0we{j}")
            ld_end[j] = s.mark()
        for fc in range(DC):
            ps = psA if fc % 2 == 0 else psB
            fa = f_a if fc % 2 == 0 else scr2k
            def emm(t, fc=fc, ps=ps):
                r = None
                for tt in range(4):
                    for kd in range(DC):
                        r = t.matmul(ps[:, tt * 512:(tt + 1) * 512],
                                     wexp_sl[fc % 2][:, kd * P:(kd + 1) * P],
                                     xn_fm[:, kd * T + tt * 512: kd * T + (tt + 1) * 512],
                                     start=(kd == 0), stop=(kd == DC - 1))
                return r
            s.step("tensor", emm, wait=Seq.merge(ld_end[fc], ev_end.get(fc - 2)))
            emm_end[fc] = s.mark()
            s.step("scalar", lambda sc, fc=fc, ps=ps, fa=fa: sc.activation(
                fa[:], ps[:], AF.Identity, bias=be_sb[:, fc:fc + 1]), wait=emm_end[fc])
            s.step("vector", lambda v, fa=fa: v.tensor_mul(fa[:], fa[:], f_b[:]))

            def acc(v, fc=fc, fa=fa):
                xo = x_fm[:, fc * T:(fc + 1) * T]
                return v.tensor_add(xo, xo, fa[:])
            s.step("vector", acc)
            ev_end[fc] = s.mark()
            if fc + 2 < DC:
                s.dma("gpsimd", [(wexp_sl[fc % 2][:], Wap["l0_we"].ap()[fc + 2])],
                      wait=emm_end[fc], stream=f"l0we{fc % 2}")
                ld_end[fc + 2] = s.mark()

    dump = _os.environ.get("B0_DUMP", "")
    if dump == "sbc":
        s.step("vector", lambda v: v.tensor_copy(x_fm[:, 0:T], f_b[:]))
    elif dump == "sag":
        s.step("vector", lambda v: v.tensor_copy(x_fm[:, 0:T], sag_sb))
    elif dump == "thr":
        def dthr(v):
            for i in range(16):
                v.tensor_copy(x_fm[:, i:i+1], thr_col[:])
            return v.tensor_copy(x_fm[:, 16:17], cnt_col[:])
        s.step("vector", dthr)
    elif dump == "stok":
        s.step("vector", lambda v: v.tensor_copy(x_fm[:, 0:256], s_tok))
    elif dump == "xn":
        s.step("vector", lambda v: v.tensor_copy(x_fm[:], xn_fm[:]))
    elif dump == "fs":
        s.step("vector", lambda v: v.tensor_copy(x_fm[:], xn_fm[:]))
    elif dump == "g0":
        s.step("vector", lambda v: v.tensor_copy(x_fm[:, 0:2 * T], xn_fm[:, 0:2 * T]))
    elif dump == "av0":
        s.step("vector", lambda v: v.tensor_copy(x_fm[:, 0:2 * T], xn_fm[:, 0:2 * T]))
    elif dump == "qk":
        s.step("vector", lambda v: v.tensor_copy(x_fm[:, 0:T], qt_pair[:]))
        s.step("vector", lambda v: v.tensor_copy(x_fm[:, T:2 * T], kt_pair[:]))
        s.step("vector", lambda v: v.tensor_copy(x_fm[:, 2 * T:2 * T + 4], nrm[:]))
        s.step("vector", lambda v: v.tensor_copy(
            x_fm[:, 2 * T + 4:2 * T + 4 + 2080], v_aug[:]))
    elif dump == "attnT":
        s.step("vector", lambda v: v.tensor_copy(x_fm[:, 0:T], attnT[:, 0:T]))
        s.step("vector", lambda v: v.tensor_copy(x_fm[:, T:2 * T], attnT[:, T:2 * T]))



def prep_l1(inp):
    w = {}
    for nm, key in (("q", "wq"), ("k", "wk"), ("v", "wv")):
        codes = np.asarray(inp[f"L1_attn_{key}_codes"], dtype=np.float32)
        w[f"l1_{nm}c"] = tile_k_major(codes, P).astype(bfloat16)
        sc = np.asarray(inp[f"L1_attn_{key}_scales"], dtype=np.float32)
        scl = sc.reshape(DC, 4, DC, P).transpose(2, 1, 0, 3).reshape(DC, 4, DC * P)
        w[f"l1_{nm}s"] = np.ascontiguousarray(np.repeat(scl, 32, axis=1)).astype(bfloat16)
    woc = np.asarray(inp["L1_attn_wo_codes"], dtype=np.float32)
    w["l1_oc"] = np.ascontiguousarray(
        woc.reshape(8, 2, P, DC, P).transpose(0, 2, 3, 1, 4).reshape(8, P, DC * 2 * P)).astype(bfloat16)
    wos = np.asarray(inp["L1_attn_wo_scales"], dtype=np.float32)
    so = wos.reshape(8, 2, 4, DC, P).transpose(0, 2, 3, 1, 4).reshape(8, 4, DC * 2 * P)
    w["l1_os"] = np.ascontiguousarray(np.repeat(so, 32, axis=1)).astype(bfloat16)
    w["l1_bq"] = np.ascontiguousarray(
        np.asarray(inp["L1_attn_bq"], dtype=np.float32).reshape(DC, P).T)
    w["l1_bk"] = np.ascontiguousarray(
        np.asarray(inp["L1_attn_bk"], dtype=np.float32).reshape(DC, P).T)
    w["l1_bv"] = np.asarray(inp["L1_attn_bv"], dtype=np.float32).reshape(1, 2048).copy()
    w["l1_bo"] = np.ascontiguousarray(
        np.asarray(inp["L1_attn_bo"], dtype=np.float32).reshape(DC, P).T)
    r4 = np.zeros((4, P), dtype=np.float32)
    r4[np.arange(P) // 32, np.arange(P)] = 1.0
    w["l1_rep4"] = r4
    clT = np.ascontiguousarray(np.asarray(inp["L1_ffn_classifier_w"], dtype=np.float32).T)
    w["l1_cl"] = tile_k_major(clT, 12)[0]
    w["l1_ec"] = np.asarray(inp["L1_ffn_e_score_correction_bias"], dtype=np.float32).reshape(1, 12).copy()
    ssk = np.zeros((12, 1), dtype=np.float32)
    ssk[8:12] = 1.0
    w["l1_ssk"] = ssk
    gwr = np.asarray(inp["L1_ffn_gate_w"], dtype=np.float32)
    w["l1_gw"] = np.concatenate([tile_k_major(np.ascontiguousarray(gwr[e].T), P)
                                 for e in range(8)], axis=0).astype(bfloat16)
    uwr = np.asarray(inp["L1_ffn_up_w"], dtype=np.float32)
    w["l1_uw"] = np.concatenate([tile_k_major(np.ascontiguousarray(uwr[e].T), P)
                                 for e in range(8)], axis=0).astype(bfloat16)
    dwr = np.asarray(inp["L1_ffn_down_w"], dtype=np.float32)
    w["l1_dw"] = np.concatenate([tile_k_major(np.ascontiguousarray(dwr[e].T), P)
                                 for e in range(8)], axis=0).astype(bfloat16)
    return w


def prep_l3(inp):
    w = {}
    gwT = np.ascontiguousarray(np.asarray(inp["L3_ffn_gate_w"], dtype=np.int32).T)
    uwT = np.ascontiguousarray(np.asarray(inp["L3_ffn_up_w"], dtype=np.int32).T)
    dwT = np.ascontiguousarray(np.asarray(inp["L3_ffn_down_w"], dtype=np.int32).T)
    w["l3_gw"] = tile_k_major(gwT, P).astype(bfloat16)
    w["l3_uw"] = tile_k_major(uwT, P).astype(bfloat16)
    w["l3_dw"] = tile_k_major(dwT, P).astype(bfloat16)
    w["l3_gs"] = np.ascontiguousarray(
        np.asarray(inp["L3_ffn_gate_scale"], dtype=np.float32).reshape(64, P).T)
    w["l3_us"] = np.ascontiguousarray(
        np.asarray(inp["L3_ffn_up_scale"], dtype=np.float32).reshape(64, P).T)
    w["l3_ds"] = np.ascontiguousarray(
        np.asarray(inp["L3_ffn_down_scale"], dtype=np.float32).reshape(DC, P).T)
    ipT = np.ascontiguousarray(np.asarray(inp["L3_attn_in_proj_w"], dtype=np.float32).T)
    w["l3_ip"] = tile_k_major(np.ascontiguousarray(ipT[:, :8320]), P).astype(bfloat16)
    w["l3_ipdt"] = tile_k_major(np.ascontiguousarray(ipT[:, 8320:]), 32)[0].astype(bfloat16)
    opT = np.ascontiguousarray(np.asarray(inp["L3_attn_o_proj_w"], dtype=np.float32).T)
    w["l3_op"] = tile_k_major(opT, P).astype(bfloat16)
    cw = np.asarray(inp["L3_attn_conv_w"], dtype=np.float32)
    w["l3_cw"] = np.ascontiguousarray(cw.reshape(33, P, 4).transpose(1, 0, 2).reshape(P, 33 * 4))
    w["l3_cb"] = np.ascontiguousarray(
        np.asarray(inp["L3_attn_conv_b"], dtype=np.float32).reshape(33, P).T)
    w["l3_alog"] = np.asarray(inp["L3_attn_A_log"], dtype=np.float32).reshape(32, 1).copy()
    w["l3_dtb"] = np.asarray(inp["L3_attn_dt_bias"], dtype=np.float32).reshape(32, 1).copy()
    w["l3_dsk"] = np.asarray(inp["L3_attn_D_skip"], dtype=np.float32).reshape(32, 1).copy()
    w["l3_nw"] = np.ascontiguousarray(
        np.asarray(inp["L3_attn_norm_w"], dtype=np.float32).reshape(32, P).T)
    hb = np.zeros((32, 32 * P), dtype=np.float32)
    for h in range(32):
        hb[h, h * P:(h + 1) * P] = 1.0
    w["l3_hb32"] = np.ascontiguousarray(hb.reshape(32, 8, 512).transpose(1, 0, 2))
    j = np.arange(P)[:, None]
    i = np.arange(256)[None, :]
    dm = np.concatenate([(0 * 128 + j <= i).astype(np.float32),
                         (1 * 128 + j <= i).astype(np.float32)], axis=1)
    w["l3_cmask"] = dm.astype(bfloat16)
    return w


def emit_block1(bk, cm, Wap):
    nc = bk.nc
    s = bk.seq
    x_fm, xn_fm = cm["x_fm"], cm["xn_fm"]
    psA, psB = cm["psA"], cm["psB"]
    ones, ident = cm["ones"], cm["ident"]
    identf = cm["ident_f32"]
    onesf = cm["onesf_row"]
    f_a, f_b = cm["f2k_a"], cm["f2k_b"]
    SC = 0.08838834764831845

    emit_ln(bk, cm)
    SKIP_ATTN = _os.environ.get("B1_SKIP_ATTN", "") == "1"
    SKIP_FFN = _os.environ.get("B1_SKIP_FFN", "") == "1"

    if not SKIP_ATTN:
        bk.push_scope()
        wq_bf = bk.sb("b1wq", [P, DC * P], BF16)
        wk_bf = bk.sb("b1wk", [P, DC * P], BF16)
        wv_bf = bk.sb("b1wv", [P, DC * P], BF16)
        wcd = [bk.sb(f"b1wc{i}", [P, DC * P], BF16) for i in range(2)]
        srp = [bk.sb(f"b1sr{i}", [P, DC * P], BF16) for i in range(2)]
        bq_sb = bk.sb("b1bq", [P, DC], F32)
        bk_sb = bk.sb("b1bk", [P, DC], F32)
        bv_sb = bk.sb("b1bv", [1, 2048], BF16)
        bo_sb = bk.sb("b1bo", [P, DC], F32)
        q_h = bk.sb("b1q", [P, T], BF16)
        k_h = bk.sb("b1k", [P, T], BF16)
        v_aug = bk.sb("b1v", [P, DC * 129], BF16)
        p_big = bk.sb("b1p", [P, 4 * 512], BF16)
        nrm = bk.sb("b1nrm", [P, 4], F32)
        attnT = bk.sb("b1attnT", [P, 2 * T], BF16)
        wo_bf = bk.sb("b1wo", [P, DC * 2 * P], BF16)
        recips = cm["sq8k"][:].bitcast(F32)[:, 128:132]
        scr2k = cm["sq8k"]
        f_a = cm["f2k_a"]

        s.dma("gpsimd", [
            (bq_sb[:], Wap["l1_bq"].ap()),
            (bk_sb[:], Wap["l1_bk"].ap()),
            (bv_sb[:], Wap["l1_bv"].ap()),
            (bo_sb[:], Wap["l1_bo"].ap()),
        ])
        s.step("vector", lambda v: v.memset(
            v_aug[:].rearrange("p (i c) -> p i c", c=129)[:, :, 128:129], 1.0))

        td = lambda: {"tensor": s.cnt["tensor"]}
        vd = lambda: {"vector": s.cnt["vector"]}
        sd = lambda: {"scalar": s.cnt["scalar"]}
        base1 = s.mark()
        dq_ld = {}
        dq_mul = {}
        dq_m = [0]
        wfree = {"q": base1, "k": base1, "v": base1, "o": base1}

        def dq_issue(kind, h, hf=0):
            m = dq_m[0]
            dq_m[0] = m + 1
            if kind == "o":
                srcs = [(wcd[m % 2][:], Wap["l1_oc"].ap()[h][:, hf * 2048:(hf + 1) * 2048]),
                        (srp[m % 2][:], Wap["l1_os"].ap()[h][:, hf * 2048:(hf + 1) * 2048])]
            else:
                srcs = [(wcd[m % 2][:], Wap[f"l1_{kind}c"].ap()[h]),
                        (srp[m % 2][:], Wap[f"l1_{kind}s"].ap()[h])]
            dq_ld[m] = s.dma("gpsimd", srcs, stream=f"b1dq{m % 2}", ghost=True,
                             wait=Seq.merge(base1, dq_mul.get(m - 2)))
            return m

        def dq_mul_emit(m, dst):
            s.step("vector", lambda v, m=m, dst=dst: v.tensor_mul(
                dst, wcd[m % 2][:], srp[m % 2][:]),
                wait=Seq.merge(dq_ld[m], wfree_for[m]))
            dq_mul[m] = vd()

        wfree_for = {}
        for h in range(int(_os.environ.get("B1X_H", str(DC)))):
            wfree_for[dq_m[0]] = wfree["q"]
            mq = dq_issue("q", h)
            dq_mul_emit(mq, wq_bf[:])
            wfree_for[dq_m[0]] = wfree["k"]
            mk = dq_issue("k", h)
            dq_mul_emit(mk, wk_bf[:])
            wfree_for[dq_m[0]] = wfree["v"]
            mv = dq_issue("v", h)
            dq_mul_emit(mv, wv_bf[:])
            def projq(t):
                r = None
                for tt in range(4):
                    for kd in range(DC):
                        r = t.matmul(psA[:, tt * 512:(tt + 1) * 512],
                                     wq_bf[:, kd * P:(kd + 1) * P],
                                     xn_fm[:, kd * T + tt * 512: kd * T + (tt + 1) * 512],
                                     start=(kd == 0), stop=(kd == DC - 1))
                return r
            s.step("tensor", projq, wait=Seq.merge(dq_mul[mq], wfree["psA"] if h else base1))
            projq_d = td()
            wfree["q"] = projq_d
            s.step("scalar", lambda sc, h=h: sc.activation(
                q_h[:], psA[:], AF.Identity, bias=bq_sb[:, h:h + 1]), wait=projq_d)
            evq_d = sd()
            def projk(t):
                r = None
                for tt in range(4):
                    for kd in range(DC):
                        r = t.matmul(psB[:, tt * 512:(tt + 1) * 512],
                                     wk_bf[:, kd * P:(kd + 1) * P],
                                     xn_fm[:, kd * T + tt * 512: kd * T + (tt + 1) * 512],
                                     start=(kd == 0), stop=(kd == DC - 1))
                return r
            s.step("tensor", projk, wait=Seq.merge(dq_mul[mk], wfree["psB"] if h else base1))
            projk_d = td()
            wfree["k"] = projk_d
            s.step("scalar", lambda sc, h=h: sc.activation(
                k_h[:], psB[:], AF.Identity, bias=bk_sb[:, h:h + 1]), wait=projk_d)
            evk_d = sd()
            vev_d = {}
            _B1PH0 = _os.environ.get("B1X_PH", "dpvbsq")
            for grp in (range(4) if "v" in _B1PH0 else []):
                def vproj(t, grp=grp, h=h):
                    r = None
                    for j in range(4):
                        ti = grp * 4 + j
                        dstp = psB[:, j * 512: j * 512 + P]
                        for kd in range(DC):
                            t.matmul(dstp, xn_fm[:, kd * T + ti * P: kd * T + (ti + 1) * P],
                                     wv_bf[:, kd * P:(kd + 1) * P],
                                     start=(kd == 0), stop=False)
                        r = t.matmul(dstp, ones[0:1, :], bv_sb[0:1, h * P:(h + 1) * P],
                                     start=False, stop=True)
                    return r
                s.step("tensor", vproj, wait=Seq.merge(
                    dq_mul[mv], evk_d if grp == 0 else None, vev_d.get(grp - 1)))
                vproj_d = td()
                wfree["v"] = vproj_d
                def vevac(sc, grp=grp):
                    dst = v_aug[:].rearrange("p (i c) -> p i c", c=129)[
                        :, grp * 4:(grp + 1) * 4, 0:128]
                    src = psB[:].rearrange("p (i c) -> p i c", c=512)[:, :, 0:128]
                    return sc.activation(dst, src, AF.Copy)
                s.step("scalar", vevac, wait=vproj_d)
                vev_d[grp] = sd()
            if "b" not in _os.environ.get("B1X_PH", "dpvbsq"):
                raise SystemExit("B1X no-b unsupported")
            s.step("vector", lambda v: v.tensor_mul(scr2k[:], q_h[:], q_h[:]), wait=evq_d)
            sqq_d = vd()
            def nmmq(t):
                r = None
                for tt in range(4):
                    r = t.matmul(psA[:, tt * 512:(tt + 1) * 512], ones[:],
                                 scr2k[:, tt * 512:(tt + 1) * 512], start=True, stop=True)
                return r
            s.step("tensor", nmmq, wait=sqq_d)
            nmmq_d = td()
            s.step("vector", lambda v: v.tensor_mul(scr2k[:], k_h[:], k_h[:]),
                   wait=Seq.merge(nmmq_d, evk_d))
            sqk_d = vd()
            def nmmk(t):
                r = None
                for tt in range(4):
                    r = t.matmul(psB[:, tt * 512:(tt + 1) * 512], ones[:],
                                 scr2k[:, tt * 512:(tt + 1) * 512], start=True, stop=True)
                return r
            s.step("tensor", nmmk, wait=Seq.merge(sqk_d, vev_d.get(3)))
            nmmk_d = td()
            def nred(v):
                v.reduce_max(nrm[:, 0:1], psA[:], axis=AX.X)
                return v.reduce_max(nrm[:, 1:2], psB[:], axis=AX.X)
            s.step("vector", nred, wait=nmmk_d)
            s.step("vector", lambda v: v.tensor_mul(nrm[:, 2:3], nrm[:, 0:1], nrm[:, 1:2]),
                   wait=vd())
            s.step("scalar", lambda sc: sc.activation(nrm[:, 3:4], nrm[:, 2:3],
                                                      AF.Sqrt, scale=SC * SC), wait=vd())
            s.step("vector", lambda v: v.tensor_scalar_mul(nrm[:, 0:1], nrm[:, 3:4], -1.0),
                   wait=sd())
            negb_d = vd()

            _B1PH = _os.environ.get("B1X_PH", "dpvbsq")
            exp_d = {}
            av_d = {}
            trev_d = {}
            norm2_d = {}
            def sco(t, gi):
                qt, i = gi // 8, gi % 8
                par = i % 2
                r = None
                for j2 in range(2):
                    kt = (i * 2 + j2) % 16
                    r = t.matmul(psA[:, par * 1024 + j2 * 512: par * 1024 + (j2 + 1) * 512],
                                 k_h[:, kt * P:(kt + 1) * P],
                                 q_h[:, qt * 512:(qt + 1) * 512],
                                 start=True, stop=True)
                return r
            def av(t, gi):
                qt, i = gi // 8, gi % 8
                par = i % 2
                r = None
                for j2 in range(2):
                    kt = (i * 2 + j2) % 16
                    for qs in range(4):
                        r = t.matmul(
                            psB[:, qs * 512: qs * 512 + 129],
                            p_big[:, par * 1024 + j2 * 512 + qs * 128:
                                  par * 1024 + j2 * 512 + (qs + 1) * 128],
                            v_aug[:, kt * 129:(kt + 1) * 129],
                            start=(i == 0 and j2 == 0),
                            stop=(i == 7 and j2 == 1))
                return r
            def emit_av(gi2):
                s.step("tensor", lambda t, gi2=gi2: av(t, gi2),
                       wait=Seq.merge(exp_d[gi2],
                                      norm2_d.get(gi2 // 8 - 1) if gi2 % 8 == 0 else None))
                av_d[gi2] = td()
            for qt in (range(4) if "q" in _B1PH else []):
                for i in range(8):
                    gi = qt * 8 + i
                    w_sco = Seq.merge(
                        exp_d.get(gi - 2) if gi >= 2 else negb_d,
                        trev_d.get(qt - 1) if i == 1 else None,
                        negb_d if gi == 0 else None)
                    s.step("tensor", lambda t, gi=gi: sco(t, gi), wait=w_sco)
                    sco_d = td()
                    s.step("scalar", lambda sc, gi=gi: sc.activation(
                        p_big[:, (gi % 2) * 1024:(gi % 2) * 1024 + 1024],
                        psA[:, (gi % 2) * 1024:(gi % 2) * 1024 + 1024],
                        AF.Exp, bias=nrm[:, 0:1], scale=SC),
                        wait=Seq.merge(sco_d, av_d.get(gi - 2)))
                    exp_d[gi] = sd()
                    if i >= 1:
                        emit_av(gi - 1)
                emit_av(qt * 8 + 7)
                s.step("vector", lambda v: v.reciprocal(
                    recips, psB[:].rearrange("p (q c) -> p q c", c=512)[:, :, 128:129]),
                    wait=av_d[qt * 8 + 7])
                def norm2(v):
                    r = None
                    for qs in range(4):
                        r = v.tensor_scalar(
                            scr2k[:, 1024 + qs * 128: 1024 + (qs + 1) * 128],
                            psB[:, qs * 512: qs * 512 + 128],
                            recips[:, qs:qs + 1], None, op0=ALU.mult)
                    return r
                s.step("vector", norm2, wait=vd())
                norm2_d[qt] = vd()
                def tr(t):
                    r = None
                    ptr = psA[:, 1536:2048].bitcast(BF16)
                    for qs in range(4):
                        r = t.transpose(ptr[:, qs * 128:(qs + 1) * 128],
                                        scr2k[:, 1024 + qs * 128: 1024 + (qs + 1) * 128],
                                        ident[:])
                    return r
                s.step("tensor", tr, wait=norm2_d[qt])
                tr_d = td()
                def trev(v, h=h, qt=qt):
                    ptr = psA[:, 1536:2048].bitcast(BF16)
                    dst = attnT[:, (h % 2) * T + qt * 512: (h % 2) * T + (qt + 1) * 512]
                    return v.tensor_copy(dst, ptr[:, 0:512])
                s.step("vector", trev, wait=tr_d)
                trev_d[qt] = vd()
            if exp_d:
                wfree["psA"] = Seq.merge(exp_d[31], trev_d[3])
                wfree["psB"] = trev_d[3]

            if h % 2 == 1:
                g = h // 2
                womul_d = None
                for hf in range(2):
                    wfree_for[dq_m[0]] = wfree["o"]
                    mo = dq_issue("o", g, hf)
                    s.step("vector", lambda v, mo=mo, hf=hf: v.tensor_mul(
                        wo_bf[:, hf * 2048:(hf + 1) * 2048], wcd[mo % 2][:], srp[mo % 2][:]),
                        wait=Seq.merge(dq_ld[mo], wfree_for[mo]))
                    dq_mul[mo] = vd()
                    womul_d = dq_mul[mo]
                acc_d = {}
                omm_d = {}
                for Dc in range(DC):
                    ps = psA if Dc % 2 == 0 else psB
                    def omm(t, Dc=Dc, ps=ps):
                        r = None
                        for tt in range(4):
                            for fc in range(2):
                                r = t.matmul(ps[:, tt * 512:(tt + 1) * 512],
                                             wo_bf[:, (Dc * 2 + fc) * P:(Dc * 2 + fc + 1) * P],
                                             attnT[:, fc * T + tt * 512: fc * T + (tt + 1) * 512],
                                             start=(fc == 0), stop=(fc == 1))
                        return r
                    s.step("tensor", omm, wait=Seq.merge(
                        womul_d, wfree["psA"] if Dc == 0 else None,
                        wfree["psB"] if Dc == 1 else None, acc_d.get(Dc - 2)))
                    omm_d[Dc] = td()
                    if g == 0:
                        s.step("scalar", lambda sc, Dc=Dc, ps=ps: sc.activation(
                            f_a[:], ps[:], AF.Identity, bias=bo_sb[:, Dc:Dc + 1]),
                            wait=Seq.merge(omm_d[Dc], acc_d.get(Dc - 1)))
                        bia_d = sd()
                        def acc0(v, Dc=Dc):
                            xo = x_fm[:, Dc * T:(Dc + 1) * T]
                            return v.tensor_add(xo, xo, f_a[:])
                        s.step("vector", acc0, wait=bia_d)
                    else:
                        def accg(v, Dc=Dc, ps=ps):
                            xo = x_fm[:, Dc * T:(Dc + 1) * T]
                            return v.tensor_add(xo, xo, ps[:])
                        s.step("vector", accg, wait=omm_d[Dc])
                    acc_d[Dc] = vd()
                wfree["o"] = omm_d[DC - 1]
                wfree["psA"] = acc_d[14]
                wfree["psB"] = acc_d[15]
        bk.pop_scope()

    if SKIP_FFN:
        return
    bk.push_scope()
    h_half = bk.sb("b1h", [P, 8 * 1024], BF16)
    wg_sl = [bk.sb(f"b1wg{i}", [P, DC * P], BF16) for i in range(2)]
    wu_sl = [bk.sb(f"b1wu{i}", [P, DC * P], BF16) for i in range(2)]
    wd_sl = [bk.sb(f"b1wd{i}", [P, 8 * P], BF16) for i in range(2)]
    cl_sl = bk.sb("b1cl", [P, DC * 12], BF16)
    ec_sb = bk.sb("b1ec", [1, 12], F32)
    ssk_sb = bk.sb("b1ssk", [12, 1], BF16)
    wfT = bk.sb("b1wfT", [12, T], BF16)
    mx8 = bk.sb("b1mx", [P, 8], F32)
    scr2k = cm["sq8k"]
    big_f = scr2k[:].bitcast(F32)
    stok = big_f[:, 0:192 // 1]
    btok = big_f[:, 192:384]
    eqm = big_f[:, 384:576]
    smax = big_f[:, 576:592]
    ecb = big_f[:, 592:604]
    psA_bf = psA[:].bitcast(BF16)

    s.dma("gpsimd", [
        (cl_sl[:], Wap["l1_cl"].ap()),
        (ec_sb[:], Wap["l1_ec"].ap()),
        (ssk_sb[:], Wap["l1_ssk"].ap()),
    ])
    def clmm(t):
        r = None
        for tt in range(4):
            for kd in range(DC):
                r = t.matmul(psA[0:12, tt * 512:(tt + 1) * 512],
                             cl_sl[:, kd * 12:(kd + 1) * 12],
                             xn_fm[:, kd * T + tt * 512: kd * T + (tt + 1) * 512],
                             start=(kd == 0), stop=(kd == DC - 1))
        return r
    s.step("tensor", clmm)
    s.step("vector", lambda v: v.tensor_copy(f_a[0:12, 0:T], psA[0:12, 0:T]))
    def ltr(t):
        r = None
        for i in range(16):
            r = t.transpose(psB[:, i * 16:(i + 1) * 16].bitcast(BF16)[:, 0:12]
                            if False else psA_bf[:, i * 12:(i + 1) * 12],
                            f_a[0:12, i * P:(i + 1) * P], ident[0:12, 0:12])
        return r
    s.step("tensor", ltr)
    s.step("vector", lambda v: v.tensor_copy(stok, psA_bf[:, 0:192]))
    s.step("vector", lambda v: v.reduce_max(
        smax, stok.rearrange("p (i e) -> p i e", e=12), axis=AX.X))
    def subst(v):
        r = None
        for i in range(16):
            r = v.tensor_scalar(stok[:, i * 12:(i + 1) * 12], stok[:, i * 12:(i + 1) * 12],
                                smax[:, i:i + 1], None, op0=ALU.subtract)
        return r
    s.step("vector", subst)
    s.step("scalar", lambda sc: sc.activation(stok, stok, AF.Exp))
    s.step("vector", lambda v: v.reduce_sum(
        smax, stok.rearrange("p (i e) -> p i e", e=12), axis=AX.X))
    s.step("vector", lambda v: v.reciprocal(smax, smax))
    def nrml(v):
        r = None
        for i in range(16):
            r = v.tensor_scalar(stok[:, i * 12:(i + 1) * 12], stok[:, i * 12:(i + 1) * 12],
                                smax[:, i:i + 1], None, op0=ALU.mult)
        return r
    s.step("vector", nrml)
    s.step("tensor", lambda t: t.matmul(psB[:, 0:12], onesf[:], ec_sb[:],
                                        start=True, stop=True))
    s.step("vector", lambda v: v.tensor_copy(ecb, psB[:, 0:12]))
    def badd(v):
        r = None
        for i in range(16):
            r = v.tensor_add(btok[:, i * 12:(i + 1) * 12], stok[:, i * 12:(i + 1) * 12], ecb)
        return r
    s.step("vector", badd)
    for i in range(16):
        s.step("vector", lambda v, i=i: v.max(out=mx8[:], in_=btok[:, i * 12:(i + 1) * 12]))
        s.step("vector", lambda v, i=i: v.tensor_scalar(
            eqm[:, i * 12:(i + 1) * 12], btok[:, i * 12:(i + 1) * 12], mx8[:, 0:1], None,
            op0=ALU.is_equal))
        s.step("vector", lambda v, i=i: v.tensor_scalar(
            f_b[:, 0:12], btok[:, i * 12:(i + 1) * 12], mx8[:, 1:2], None, op0=ALU.is_equal))
        s.step("vector", lambda v, i=i: v.tensor_add(
            eqm[:, i * 12:(i + 1) * 12], eqm[:, i * 12:(i + 1) * 12], f_b[:, 0:12]))
        s.step("vector", lambda v, i=i: v.tensor_mul(
            eqm[:, i * 12:(i + 1) * 12], eqm[:, i * 12:(i + 1) * 12],
            stok[:, i * 12:(i + 1) * 12]))
        s.step("vector", lambda v, i=i: v.tensor_scalar_mul(
            eqm[:, i * 12:(i + 1) * 12], eqm[:, i * 12:(i + 1) * 12], 1.5))
    s.step("vector", lambda v: v.tensor_copy(scr2k[:, 3072:3072 + 192].bitcast(BF16)
                                             if False else f_b[:, 0:192], eqm))
    def wtr(t):
        r = None
        for i in range(16):
            r = t.transpose(psA_bf[0:12, i * P:(i + 1) * P],
                            f_b[:, i * 12:(i + 1) * 12], ident[:])
        return r
    s.step("tensor", wtr)
    s.step("vector", lambda v: v.tensor_copy(wfT[:], psA_bf[0:12, 0:T]))
    def skr(t):
        r = None
        for tt in range(4):
            r = t.matmul(psB[0:1, tt * 512:(tt + 1) * 512], ssk_sb[:],
                         wfT[:, tt * 512:(tt + 1) * 512], start=True, stop=True)
        return r
    s.step("tensor", skr)
    s.step("vector", lambda v: v.tensor_copy(f_a[0:1, 0:T], psB[0:1, 0:T]))
    def skb(t):
        r = None
        for tt in range(4):
            r = t.matmul(psB[:, tt * 512:(tt + 1) * 512], ones[0:1, :],
                         f_a[0:1, tt * 512:(tt + 1) * 512], start=True, stop=True)
        return r
    s.step("tensor", skb)
    s.step("vector", lambda v: v.tensor_copy(f_b[:], psB[:]))
    for kd in range(DC):
        def sk1(v, kd=kd):
            return v.tensor_mul(f_a[:], xn_fm[:, kd * T:(kd + 1) * T], f_b[:])
        s.step("vector", sk1)
        def sk2(v, kd=kd):
            xo = x_fm[:, kd * T:(kd + 1) * T]
            return v.tensor_add(xo, xo, f_a[:])
        s.step("vector", sk2)
    for e in range(8):
        def srow(t, e=e):
            r = None
            for tt in range(4):
                r = t.matmul(psB[0:1, tt * 512:(tt + 1) * 512], ident[0:12, e:e + 1],
                             wfT[:, tt * 512:(tt + 1) * 512], start=True, stop=True)
            return r
        s.step("tensor", srow)
        s.step("vector", lambda v: v.tensor_copy(f_a[0:1, 0:T], psB[0:1, 0:T]))
        def sbc(t):
            r = None
            for tt in range(4):
                r = t.matmul(psB[:, tt * 512:(tt + 1) * 512], ones[0:1, :],
                             f_a[0:1, tt * 512:(tt + 1) * 512], start=True, stop=True)
            return r
        s.step("tensor", sbc)
        s.step("vector", lambda v: v.tensor_copy(f_b[:], psB[:]))
        for half in range(2):
            t0 = half * 1024
            ld_end = {}
            silu_end = {}
            tt_end = {}
            mmu_end = {}
            base_h = s.mark()
            for j in range(2):
                ld_end[j] = s.dma("gpsimd", [
                    (wg_sl[j][:], Wap["l1_gw"].ap()[e * 8 + j]),
                    (wu_sl[j][:], Wap["l1_uw"].ap()[e * 8 + j]),
                ], stream=f"b1gu{j}", ghost=True, wait=base_h)
            for ic in range(8):
                po = (ic % 2) * 1024
                def mmg(t, t0=t0, ic=ic, po=po):
                    r = None
                    for tt in range(2):
                        for kd in range(DC):
                            r = t.matmul(psA[:, po + tt * 512: po + (tt + 1) * 512],
                                         wg_sl[ic % 2][:, kd * P:(kd + 1) * P],
                                         xn_fm[:, kd * T + t0 + tt * 512:
                                               kd * T + t0 + (tt + 1) * 512],
                                         start=(kd == 0), stop=(kd == DC - 1))
                    return r
                s.step("tensor", mmg, wait=Seq.merge(base_h, ld_end[ic],
                                                     silu_end.get(ic - 2)))
                mg = {"tensor": s.cnt["tensor"]}
                def mmu(t, t0=t0, ic=ic, po=po):
                    r = None
                    for tt in range(2):
                        for kd in range(DC):
                            r = t.matmul(psB[:, po + tt * 512: po + (tt + 1) * 512],
                                         wu_sl[ic % 2][:, kd * P:(kd + 1) * P],
                                         xn_fm[:, kd * T + t0 + tt * 512:
                                               kd * T + t0 + (tt + 1) * 512],
                                         start=(kd == 0), stop=(kd == DC - 1))
                    return r
                s.step("tensor", mmu, wait=tt_end.get(ic - 2))
                mmu_end[ic] = {"tensor": s.cnt["tensor"]}
                s.step("scalar", lambda sc, po=po: sc.activation(
                    f_a[:, po:po + 1024], psA[:, po:po + 1024], AF.Silu), wait=mg)
                silu_end[ic] = {"scalar": s.cnt["scalar"]}
                s.step("vector", lambda v, ic=ic, po=po: v.tensor_mul(
                    h_half[:, ic * 1024:(ic + 1) * 1024], f_a[:, po:po + 1024],
                    psB[:, po:po + 1024]),
                    wait=Seq.merge(silu_end[ic], mmu_end[ic]))
                tt_end[ic] = {"vector": s.cnt["vector"]}
                if ic + 2 < 8:
                    ld_end[ic + 2] = s.dma("gpsimd", [
                        (wg_sl[ic % 2][:], Wap["l1_gw"].ap()[e * 8 + ic + 2]),
                        (wu_sl[ic % 2][:], Wap["l1_uw"].ap()[e * 8 + ic + 2]),
                    ], wait=mmu_end[ic], stream=f"b1gu{ic % 2}", ghost=True)
            h_done = Seq.merge(tt_end[6], tt_end[7])
            dld = {}
            for j in range(2):
                dld[j] = s.dma("gpsimd", [(wd_sl[j][:], Wap["l1_dw"].ap()[e * DC + j])],
                               wait=mmu_end[6 + j], stream=f"b1wd{j}", ghost=True)
            dmm_end = {}
            ev_end = {}
            for Hc in range(DC):
                po = (Hc % 2) * 1024
                ps = psA if Hc % 4 < 2 else psB
                fa = f_a[:, 0:1024] if Hc % 2 == 0 else f_a[:, 1024:2048]
                def dmm(t, ps=ps, Hc=Hc, po=po):
                    r = None
                    for tt in range(2):
                        for ic in range(8):
                            r = t.matmul(ps[:, po + tt * 512: po + (tt + 1) * 512],
                                         wd_sl[Hc % 2][:, ic * P:(ic + 1) * P],
                                         h_half[:, ic * 1024 + tt * 512:
                                                ic * 1024 + (tt + 1) * 512],
                                         start=(ic == 0), stop=(ic == 7))
                    return r
                s.step("tensor", dmm, wait=Seq.merge(dld[Hc], h_done, ev_end.get(Hc - 4)))
                dmm_end[Hc] = {"tensor": s.cnt["tensor"]}
                s.step("vector", lambda v, t0=t0, ps=ps, fa=fa, po=po: v.tensor_mul(
                    fa, ps[:, po:po + 1024], f_b[:, t0:t0 + 1024]),
                    wait=Seq.merge(dmm_end[Hc], ev_end.get(Hc - 2)))
                def accx(v, Hc=Hc, t0=t0, fa=fa):
                    xo = x_fm[:, Hc * T + t0: Hc * T + t0 + 1024]
                    return v.tensor_add(xo, xo, fa)
                s.step("vector", accx)
                ev_end[Hc] = {"vector": s.cnt["vector"]}
                if Hc + 2 < DC:
                    dld[Hc + 2] = s.dma("gpsimd", [(wd_sl[Hc % 2][:],
                                        Wap["l1_dw"].ap()[e * DC + Hc + 2])],
                                        wait=dmm_end[Hc], stream=f"b1wd{Hc % 2}",
                                        ghost=True)
    bk.pop_scope()


def emit_block3(bk, cm, Wap):
    emit_ln(bk, cm)
    if _os.environ.get("B3_SKIP_ATTN", "") != "1":
        emit_block3_attn(bk, cm, Wap)
    if _os.environ.get("B3_SKIP_FFN", "") != "1":
        emit_block3_ffn(bk, cm, Wap)


def emit_block3_attn(bk, cm, Wap):
    _PH = _os.environ.get("B3X_PHASES", "ipcdhop")
    nc = bk.nc
    s = bk.seq
    x_fm, xn_fm = cm["x_fm"], cm["xn_fm"]
    psA, psB = cm["psA"], cm["psB"]
    ones, ident = cm["ones"], cm["ident"]
    identf = cm["ident_f32"]
    onesf = cm["onesf_row"]
    f_a, f_b = cm["f2k_a"], cm["f2k_b"]
    epsc = cm["epsc"]
    onec = cm["onec"]
    CH = 256
    NH = 32

    bk.push_scope()
    xbc = bk.sb("b3xbc", [P, 33 * 259], BF16)
    xcv = bk.sb("b3xcv", [P, 33 * CH], BF16)
    wip = [bk.sb(f"b3wip{i}", [P, DC * P], BF16) for i in range(2)]

    halo_sb = bk.sb("b3halo", [P, 33 * 3], BF16)
    cw_sb = bk.sb("b3cw", [P, 33 * 4], F32)
    cb_sb = bk.sb("b3cb", [P, 33], F32)
    alog = bk.sb("b3alog", [32, 1], F32)
    acol = bk.sb("b3acol", [32, 1], F32)
    dtb = bk.sb("b3dtb", [32, 1], F32)
    dsk = bk.sb("b3dsk", [32, 1], F32)
    nw_sb = bk.sb("b3nw", [P, 32], F32)
    Db = bk.sb("b3Db", [P, 32], F32)
    ELb = bk.sb("b3ELb", [P, 32], F32)
    dtT = bk.sb("b3dtT", [32, CH], F32)
    aT = bk.sb("b3aT", [32, CH], F32)
    aT2 = bk.sb("b3aT2", [32, CH], F32)
    expaT = None
    WT = bk.sb("b3WT", [32, CH], F32)
    a_tok = bk.sb("b3atok", [P, 64], F32)
    nega_tok = bk.sb("b3negat", [P, 64], F32)
    dt_tok = bk.sb("b3dttok", [P, 64], F32)
    W_tok = bk.sb("b3Wtok", [P, 64], F32)
    scr2k = cm["sq8k"]
    sc_sb = scr2k[:, 0:512]
    cmask = scr2k[:, 512:1024]
    xc4 = scr2k[:, 1024:2048]
    y4 = scr2k[:, 1024:2048]
    B_tok = bk.sb("b3Btok", [P, P], BF16)
    xw4 = f_b[:, 1024:2048]
    z4 = f_b[:, 0:1024]
    row2 = f_b[:].bitcast(F32)[0:1, 512:1024]
    snT4 = bk.sb("b3snT", [64, 512], BF16)
    C_sb = bk.sb("b3C", [64, CH], BF16)
    hb_sb = [bk.sb(f"b3hb{i}", [32, 512], F32) for i in range(2)]
    wop = [xbc[:, 0:32 * P], xbc[:, 32 * P:64 * P]]
    wipdt2 = bk.sb("b3ipdt", [P, 512], BF16)[:]

    S_st = bk.sb("b3S", [P, NH * 64], BF16)
    psA_bf = psA[:].bitcast(BF16)
    psB_bf = psB[:].bitcast(BF16)

    s.dma("gpsimd", [
        (wipdt2, Wap["l3_ipdt"].ap()),
        (cw_sb[:], Wap["l3_cw"].ap()),
        (cb_sb[:], Wap["l3_cb"].ap()),
        (alog[:], Wap["l3_alog"].ap()),
        (dtb[:], Wap["l3_dtb"].ap()),
        (dsk[:], Wap["l3_dsk"].ap()),
        (nw_sb[:], Wap["l3_nw"].ap()),
        (cmask, Wap["l3_cmask"].ap()),
    ])
    s.step("scalar", lambda sc: sc.activation(acol[:], alog[:], AF.Exp))
    s.step("vector", lambda v: v.tensor_scalar_mul(acol[:], acol[:], -1.0))
    s.step("tensor", lambda t: t.transpose(psB[0:1, 0:32], dsk[:], identf[0:32, 0:32]))
    s.step("vector", lambda v: v.tensor_copy(row2[0:1, 0:32], psB[0:1, 0:32]))
    s.step("tensor", lambda t: t.matmul(psB[:, 0:32], onesf[:], row2[0:1, 0:32],
                                        start=True, stop=True))
    s.step("vector", lambda v: v.tensor_copy(Db[:], psB[:, 0:32]))
    s.step("vector", lambda v: v.memset(S_st[:], 0.0))
    s.step("vector", lambda v: v.memset(xbc[:], 0.0))

    for c in range(8):
        t0 = c * CH
        if c > 0:
            s.step("vector", lambda v: v.tensor_copy(
                xbc[:].rearrange("p (b i) -> p b i", i=259)[:, :, 0:3],
                halo_sb[:].rearrange("p (b i) -> p b i", i=3)))
        ip_ld = {}
        ip_mm = {}
        ip_ev = {}
        cv_sil = {}
        for b in range(2):
            ip_ld[b] = s.dma("gpsimd", [(wip[b % 2][:], Wap["l3_ip"].ap()[32 + b])],
                             stream=f"b3ip{b % 2}", ghost=True)
        for b in (range(33) if "i" in _PH else []):
            po = (b % 4) * 512
            def ipmm(t, t0=t0, b=b, po=po):
                r = None
                for kd in range(DC):
                    r = t.matmul(psA[:, po:po + CH], wip[b % 2][:, kd * P:(kd + 1) * P],
                                 xn_fm[:, kd * T + t0: kd * T + t0 + CH],
                                 start=(kd == 0), stop=(kd == DC - 1))
                return r
            s.step("tensor", ipmm, wait=Seq.merge(ip_ld[b], ip_ev.get(b - 4)))
            ip_mm[b] = {"tensor": s.cnt["tensor"]}
            s.step("scalar", lambda sc, b=b, po=po: sc.activation(
                xbc[:, b * 259 + 3: b * 259 + 259], psA[:, po:po + CH], AF.Copy),
                wait=ip_mm[b])
            ip_ev[b] = {"scalar": s.cnt["scalar"]}
            if b + 2 < 33:
                ip_ld[b + 2] = s.dma("gpsimd",
                                     [(wip[b % 2][:], Wap["l3_ip"].ap()[32 + b + 2])],
                                     wait=ip_mm[b], stream=f"b3ip{b % 2}", ghost=True)
            if "c" in _PH and ((b + 1) % 4 == 0 or b == 32):
                bg = b // 4
                blks = list(range(bg * 4, b + 1))
                fo = (bg % 2) * 1024
                def tap0(v, blks=blks, fo=fo):
                    r = None
                    for bi, bb in enumerate(blks):
                        r = v.tensor_scalar(
                            f_a[:, fo + bi * CH: fo + (bi + 1) * CH],
                            xbc[:, bb * 259: bb * 259 + CH],
                            cw_sb[:, bb * 4: bb * 4 + 1], None, op0=ALU.mult)
                    return r
                s.step("vector", tap0, wait=Seq.merge(ip_ev[b], cv_sil.get(bg - 2)))
                for j in range(1, 4):
                    def tapj(v, blks=blks, j=j, fo=fo):
                        r = None
                        for bi, bb in enumerate(blks):
                            r = v.scalar_tensor_tensor(
                                f_a[:, fo + bi * CH: fo + (bi + 1) * CH],
                                xbc[:, bb * 259 + j: bb * 259 + j + CH],
                                cw_sb[:, bb * 4 + j: bb * 4 + j + 1],
                                f_a[:, fo + bi * CH: fo + (bi + 1) * CH],
                                op0=ALU.mult, op1=ALU.add)
                        return r
                    s.step("vector", tapj, wait={"vector": s.cnt["vector"]})
                def silu4(sc, blks=blks, fo=fo):
                    r = None
                    for bi, bb in enumerate(blks):
                        r = sc.activation(xcv[:, bb * CH:(bb + 1) * CH],
                                          f_a[:, fo + bi * CH: fo + (bi + 1) * CH],
                                          AF.Silu, bias=cb_sb[:, bb:bb + 1])
                    return r
                s.step("scalar", silu4, wait={"vector": s.cnt["vector"]})
                cv_sil[bg] = {"scalar": s.cnt["scalar"]}
        s.step("vector", lambda v: v.tensor_copy(
            halo_sb[:].rearrange("p (b i) -> p b i", i=3),
            xbc[:].rearrange("p (b i) -> p b i", i=259)[:, :, 256:259]))
        def dtmm(t, t0=t0):
            r = None
            for kd in range(DC):
                r = t.matmul(psB[0:32, 0:CH], wipdt2[:, kd * 32:(kd + 1) * 32],
                             xn_fm[:, kd * T + t0: kd * T + t0 + CH],
                             start=(kd == 0), stop=(kd == DC - 1))
            return r
        s.step("tensor", dtmm)
        s.step("scalar", lambda sc: sc.activation(dtT[:], psB[0:32, 0:CH],
                                                  AF.Identity, bias=dtb[:]))
        s.step("scalar", lambda sc: sc.activation(aT[:], dtT[:], AF.Abs))
        s.step("scalar", lambda sc: sc.activation(aT[:], aT[:], AF.Exp, scale=-1.0))
        s.step("scalar", lambda sc: sc.activation(aT[:], aT[:], AF.Ln, bias=onec[0:32, 0:1]))
        s.step("vector", lambda v: v.tensor_scalar_max(aT2[:], dtT[:], 0.0))
        s.step("vector", lambda v: v.tensor_add(dtT[:], aT[:], aT2[:]))
        s.step("vector", lambda v: v.tensor_scalar(aT[:], dtT[:], acol[:], None, op0=ALU.mult))
        cur, oth = aT, aT2
        k = 1
        while k < CH:
            def shadd(v, cur=cur, oth=oth, k=k):
                v.tensor_copy(oth[:, 0:k], cur[:, 0:k])
                return v.tensor_add(oth[:, k:CH], cur[:, k:CH], cur[:, 0:CH - k])
            s.step("vector", lambda v, cur=cur, oth=oth, k=k: v.tensor_copy(
                oth[:, 0:k], cur[:, 0:k]))
            s.step("vector", lambda v, cur=cur, oth=oth, k=k: v.tensor_add(
                oth[:, k:CH], cur[:, k:CH], cur[:, 0:CH - k]))
            cur, oth = oth, cur
            k *= 2
        if cur is not aT:
            s.step("vector", lambda v: v.tensor_copy(aT[:], aT2[:]))
        expaT = aT2
        s.step("vector", lambda v: v.tensor_scalar(WT[:], aT[:], aT[:, CH - 1:CH], None,
                                                   op0=ALU.subtract))
        s.step("scalar", lambda sc: sc.activation(WT[:], WT[:], AF.Exp, scale=-1.0))
        s.step("vector", lambda v: v.tensor_mul(WT[:], WT[:], dtT[:]))
        s.step("scalar", lambda sc: sc.activation(expaT[:], aT[:], AF.Exp))
        s.step("tensor", lambda t: t.transpose(psB[0:1, 0:32], expaT[:, CH - 1:CH],
                                               identf[0:32, 0:32]))
        s.step("vector", lambda v: v.tensor_copy(row2[0:1, 0:32], psB[0:1, 0:32]))
        s.step("tensor", lambda t: t.matmul(psB[:, 0:32], onesf[:], row2[0:1, 0:32],
                                            start=True, stop=True))
        s.step("vector", lambda v: v.tensor_copy(ELb[:], psB[:, 0:32]))
        def trp(t):
            r = None
            for (srcT, dst_off) in ((aT, 0), (dtT, 2), (WT, 4)):
                for jt in range(2):
                    r = t.transpose(psB[:, (dst_off + jt) * 32:(dst_off + jt + 1) * 32],
                                    srcT[:, jt * P:(jt + 1) * P], identf[0:32, 0:32])
            return r
        s.step("tensor", trp)
        s.step("vector", lambda v: v.tensor_copy(a_tok[:], psB[:, 0:64]))
        s.step("vector", lambda v: v.tensor_copy(dt_tok[:], psB[:, 64:128]))
        s.step("vector", lambda v: v.tensor_copy(W_tok[:], psB[:, 128:192]))
        s.step("vector", lambda v: v.tensor_scalar_mul(nega_tok[:], a_tok[:], -1.0))
        s.dma("gpsimd", [(C_sb[:], xcv[64:128, 32 * CH: 33 * CH])])
        def scmm(t):
            r = None
            for jt in range(2):
                r = t.matmul(psA[:, jt * 512: jt * 512 + CH],
                             xcv[0:64, 32 * CH + jt * P: 32 * CH + (jt + 1) * P],
                             C_sb[:], start=True, stop=True)
            return r
        s.step("tensor", scmm)
        s.step("vector", lambda v: v.tensor_copy(sc_sb[:, 0:CH], psA[:, 0:CH]))
        s.step("vector", lambda v: v.tensor_copy(sc_sb[:, CH:2 * CH], psA[:, 512:512 + CH]))
        def btr(t):
            r = None
            for jt in range(2):
                r = t.transpose(psA_bf[:, 1536 + jt * 64: 1536 + (jt + 1) * 64],
                                xcv[0:64, 32 * CH + jt * P: 32 * CH + (jt + 1) * P],
                                ident[0:64, 0:64])
            return r
        s.step("tensor", btr)
        s.step("vector", lambda v: v.tensor_copy(B_tok[:], psA_bf[:, 1536:1664]))

        z_ld = {}
        z_mm = {}
        for h in range(2):
            z_ld[h] = s.dma("gpsimd", [(wip[h % 2][:], Wap["l3_ip"].ap()[h])],
                            stream=f"b3ip{h % 2}", ghost=True)
        hb_ld = {}
        for g in range(2):
            hb_ld[g] = s.dma("gpsimd", [(hb_sb[g % 2][:], Wap["l3_hb32"].ap()[g])],
                             stream=f"b3hb{g % 2}", ghost=True)
        base = s.mark()
        prev = {k: base for k in ("expacp", "yadd", "rmsmm", "stupd", "ymul",
                                  "sigB", "sqrt", "ynrm", "stmm")}
        for g in (range(8) if "h" in _PH else []):
            h0 = 4 * g
            td = lambda: {"tensor": s.cnt["tensor"]}
            vd = lambda: {"vector": s.cnt["vector"]}
            sd = lambda: {"scalar": s.cnt["scalar"]}
            def bca(t, g=g, src=aT):
                r = None
                for hh in range(4):
                    r = t.matmul(psA[:, hh * CH:(hh + 1) * CH],
                                 hb_sb[g % 2][:, hh * P:(hh + 1) * P], src[:],
                                 start=True, stop=True)
                return r
            s.step("tensor", bca, wait=Seq.merge(base, hb_ld[g], prev["expacp"]))
            bca_d = td()
            def xtr(t, h0=h0):
                r = None
                for c2 in range(8):
                    hh, jt = c2 // 2, c2 % 2
                    r = t.transpose(psB_bf[:, c2 * P:(c2 + 1) * P],
                                    xcv[:, (h0 + hh) * CH + jt * P:
                                        (h0 + hh) * CH + (jt + 1) * P], ident[:])
                return r
            s.step("tensor", xtr, wait=prev["stupd"])
            xtr_d = td()
            def sntr(t, h0=h0):
                r = None
                for hh in range(4):
                    r = t.transpose(psB_bf[0:64, 1024 + hh * P: 1024 + (hh + 1) * P],
                                    S_st[:, (h0 + hh) * 64:(h0 + hh + 1) * 64], ident[:])
                return r
            s.step("tensor", sntr, wait=prev["ymul"])
            sntr_d = td()
            zmm_d = {}
            def zmm(t, t0=t0, h=h0, po=1536):
                r = None
                for kd in range(DC):
                    r = t.matmul(psB[:, po:po + CH],
                                 wip[h % 2][:, kd * P:(kd + 1) * P],
                                 xn_fm[:, kd * T + t0: kd * T + t0 + CH],
                                 start=(kd == 0), stop=(kd == DC - 1))
                return r
            for hh in (0, 1):
                h = h0 + hh
                po = 1536 if hh % 2 == 0 else 1792
                s.step("tensor", lambda t, h=h, po=po: zmm(t, h=h, po=po),
                       wait=Seq.merge(z_ld.get(h), prev["sigB"] if hh == 0 else None))
                zmm_d[hh] = td()
                if h + 2 < NH:
                    z_ld[h + 2] = s.dma(
                        "gpsimd", [(wip[h % 2][:], Wap["l3_ip"].ap()[h + 2])],
                        wait=zmm_d[hh], stream=f"b3ip{h % 2}", ghost=True)
            def gexp(sc, h0=h0):
                r = None
                for c2 in range(8):
                    hh, jt = c2 // 2, c2 % 2
                    r = sc.activation(
                        f_a[:, c2 * CH:(c2 + 1) * CH],
                        psA[:, hh * CH:(hh + 1) * CH], AF.Exp,
                        bias=nega_tok[:, jt * 32 + h0 + hh: jt * 32 + h0 + hh + 1])
                return r
            s.step("scalar", gexp, wait=Seq.merge(bca_d, prev["rmsmm"], prev["yadd"]))
            gexp_d = sd()
            def sigA(sc):
                r = None
                for k in (0, 1):
                    po2 = 1536 if k % 2 == 0 else 1792
                    r = sc.activation(z4[:, k * CH:(k + 1) * CH],
                                      psB[:, po2:po2 + CH], AF.Sigmoid)
                return r
            s.step("scalar", sigA, wait=Seq.merge(zmm_d[1], prev["ynrm"]))
            sigA_d = sd()
            def gmin(v):
                r = None
                for c2 in range(8):
                    jt = c2 % 2
                    r = v.scalar_tensor_tensor(
                        f_a[:, c2 * CH:(c2 + 1) * CH], f_a[:, c2 * CH:(c2 + 1) * CH], 1.0,
                        sc_sb[:, jt * CH:(jt + 1) * CH], op0=ALU.min, op1=ALU.mult)
                return r
            s.step("vector", gmin, wait=gexp_d)
            gmin_d = vd()
            def gmask(v, h0=h0):
                r = None
                for c2 in range(8):
                    hh, jt = c2 // 2, c2 % 2
                    r = v.scalar_tensor_tensor(
                        f_a[:, c2 * CH:(c2 + 1) * CH], cmask[:, jt * CH:(jt + 1) * CH],
                        dt_tok[:, jt * 32 + h0 + hh: jt * 32 + h0 + hh + 1],
                        f_a[:, c2 * CH:(c2 + 1) * CH], op0=ALU.mult, op1=ALU.mult)
                return r
            s.step("vector", gmask, wait=gmin_d)
            gmask_d = vd()
            s.step("vector", lambda v: v.tensor_copy(xc4, psB_bf[:, 0:1024]),
                   wait=Seq.merge(xtr_d, prev["ynrm"]))
            xtrc_d = vd()
            s.step("vector", lambda v: v.tensor_copy(snT4[:], psB_bf[0:64, 1024:1536]),
                   wait=sntr_d)
            snc_d = vd()
            def xwm(v, h0=h0):
                r = None
                for c2 in range(8):
                    hh, jt = c2 // 2, c2 % 2
                    r = v.tensor_scalar(
                        xw4[:, c2 * P:(c2 + 1) * P], xc4[:, c2 * P:(c2 + 1) * P],
                        W_tok[:, jt * 32 + h0 + hh: jt * 32 + h0 + hh + 1], None,
                        op0=ALU.mult)
                return r
            s.step("vector", xwm, wait=Seq.merge(xtrc_d, prev["stmm"]))
            xwm_d = vd()
            def yimm(t):
                r = None
                for hh in range(4):
                    for jt in range(2):
                        c2 = hh * 2 + jt
                        r = t.matmul(psA[:, 1024 + hh * CH: 1024 + (hh + 1) * CH],
                                     xc4[:, c2 * P:(c2 + 1) * P],
                                     f_a[:, c2 * CH:(c2 + 1) * CH],
                                     start=(jt == 0), stop=(jt == 1))
                return r
            s.step("tensor", yimm, wait=Seq.merge(gmask_d, xtrc_d, prev["sqrt"]))
            yimm_d = td()
            def stmm(t):
                r = None
                for hh in range(4):
                    for jt in range(2):
                        r = t.matmul(psB[:, hh * 64:(hh + 1) * 64],
                                     xw4[:, (hh * 2 + jt) * P:(hh * 2 + jt + 1) * P],
                                     B_tok[:, jt * 64:(jt + 1) * 64],
                                     start=(jt == 0), stop=(jt == 1))
                return r
            s.step("tensor", stmm, wait=Seq.merge(xwm_d, prev["stupd"]))
            stmm_d = td()
            def yint(t):
                r = None
                for hh in range(4):
                    r = t.matmul(psB[:, 512 + hh * CH: 512 + (hh + 1) * CH],
                                 snT4[:, hh * P:(hh + 1) * P], C_sb[:],
                                 start=True, stop=True)
                return r
            s.step("tensor", yint, wait=Seq.merge(snc_d, prev["ymul"]))
            yint_d = td()
            s.step("tensor", lambda t, g=g: bca(t, g=g, src=expaT), wait=gexp_d)
            bca2_d = td()
            if g + 2 < 8:
                hb_ld[g + 2] = s.dma(
                    "gpsimd", [(hb_sb[g % 2][:], Wap["l3_hb32"].ap()[g + 2])],
                    wait=bca2_d, stream=f"b3hb{g % 2}", ghost=True)
            for hh in (2, 3):
                h = h0 + hh
                po = 1536 if hh % 2 == 0 else 1792
                s.step("tensor", lambda t, h=h, po=po: zmm(t, h=h, po=po),
                       wait=Seq.merge(z_ld.get(h), sigA_d if hh == 2 else None))
                zmm_d[hh] = td()
                if h + 2 < NH:
                    z_ld[h + 2] = s.dma(
                        "gpsimd", [(wip[h % 2][:], Wap["l3_ip"].ap()[h + 2])],
                        wait=zmm_d[hh], stream=f"b3ip{h % 2}", ghost=True)
            def sigB(sc):
                r = None
                for k in (2, 3):
                    po2 = 1536 if k % 2 == 0 else 1792
                    r = sc.activation(z4[:, k * CH:(k + 1) * CH],
                                      psB[:, po2:po2 + CH], AF.Sigmoid)
                return r
            s.step("scalar", sigB, wait=zmm_d[3])
            sigB_d = sd()
            def stupd(v, h0=h0):
                r = None
                for hh in range(4):
                    h = h0 + hh
                    r = v.scalar_tensor_tensor(
                        S_st[:, h * 64:(h + 1) * 64], S_st[:, h * 64:(h + 1) * 64],
                        ELb[:, h:h + 1], psB[:, hh * 64:(hh + 1) * 64],
                        op0=ALU.mult, op1=ALU.add)
                return r
            s.step("vector", stupd, wait=stmm_d)
            stupd_d = vd()
            s.step("vector", lambda v: v.tensor_copy(f_a[:, 1024:2048], psA[:, 0:1024]),
                   wait=Seq.merge(bca2_d, yimm_d))
            expacp_d = vd()
            s.step("vector", lambda v: v.tensor_mul(f_a[:, 1024:2048], psB[:, 512:1536],
                                                    f_a[:, 1024:2048]),
                   wait=Seq.merge(expacp_d, yint_d))
            ymul_d = vd()
            s.step("vector", lambda v: v.tensor_add(y4, f_a[:, 1024:2048],
                                                    psA[:, 1024:2048]),
                   wait=Seq.merge(ymul_d, yimm_d, stmm_d))
            yadd_d = vd()
            def dxc(v, h0=h0):
                r = None
                for hh in range(4):
                    h = h0 + hh
                    r = v.scalar_tensor_tensor(
                        y4[:, hh * CH:(hh + 1) * CH], xcv[:, h * CH:(h + 1) * CH],
                        Db[:, h:h + 1], y4[:, hh * CH:(hh + 1) * CH],
                        op0=ALU.mult, op1=ALU.add)
                return r
            s.step("vector", dxc, wait=yadd_d)
            dxc_d = vd()
            s.step("vector", lambda v: v.tensor_mul(y4, y4, z4),
                   wait=Seq.merge(dxc_d, sigB_d))
            ygate_d = vd()
            s.step("vector", lambda v: v.tensor_mul(f_a[:, 0:1024], y4, y4),
                   wait=Seq.merge(ygate_d, yimm_d))
            ysq_d = vd()
            def rmsmm(t):
                r = None
                for hh in range(4):
                    r = t.matmul(psA[:, 1024 + hh * CH: 1024 + (hh + 1) * CH], ones[:],
                                 f_a[:, hh * CH:(hh + 1) * CH], start=True, stop=True)
                return r
            s.step("tensor", rmsmm, wait=ysq_d)
            rmsmm_d = td()
            s.step("scalar", lambda sc: sc.activation(
                z4[:], psA[:, 1024:2048], AF.Sqrt, scale=1.0 / P, bias=epsc[:, 1:2]),
                wait=rmsmm_d)
            sqrt_d = sd()
            s.step("vector", lambda v: v.reciprocal(z4[:], z4[:]), wait=sqrt_d)
            recip_d = vd()
            def ynrm(v, h0=h0):
                r = None
                for hh in range(4):
                    h = h0 + hh
                    r = v.scalar_tensor_tensor(
                        xcv[:, h * CH:(h + 1) * CH], y4[:, hh * CH:(hh + 1) * CH],
                        nw_sb[:, h:h + 1], z4[:, hh * CH:(hh + 1) * CH],
                        op0=ALU.mult, op1=ALU.mult)
                return r
            s.step("vector", ynrm, wait=recip_d)
            ynrm_d = vd()
            prev = {"expacp": expacp_d, "yadd": yadd_d, "rmsmm": rmsmm_d,
                    "stupd": stupd_d, "ymul": ymul_d, "sigB": sigB_d,
                    "sqrt": sqrt_d, "ynrm": ynrm_d, "stmm": stmm_d}
        op_ld = {}
        op_mm = {}
        op_ev = {}
        for Dc in range(2):
            s.dma("gpsimd", [(wop[Dc % 2][:], Wap["l3_op"].ap()[Dc])],
                  stream=f"b3op{Dc % 2}")
            op_ld[Dc] = s.mark()
        for Dc in (range(DC) if "o" in _PH else []):
            po = (Dc % 4) * 512
            def omm(t, Dc=Dc, po=po):
                r = None
                for hc in range(32):
                    r = t.matmul(psA[:, po:po + CH], wop[Dc % 2][:, hc * P:(hc + 1) * P],
                                 xcv[:, hc * CH:(hc + 1) * CH],
                                 start=(hc == 0), stop=(hc == 31))
                return r
            s.step("tensor", omm, wait=Seq.merge(op_ld[Dc], op_ev.get(Dc - 4)))
            op_mm[Dc] = {"tensor": s.cnt["tensor"]}
            def accx(v, Dc=Dc, t0=t0, po=po):
                xo = x_fm[:, Dc * T + t0: Dc * T + t0 + CH]
                return v.tensor_add(xo, xo, psA[:, po:po + CH])
            s.step("vector", accx, wait=op_mm[Dc])
            op_ev[Dc] = {"vector": s.cnt["vector"]}
            if Dc + 2 < DC:
                op_ld[Dc + 2] = s.dma("gpsimd",
                                      [(wop[Dc % 2][:], Wap["l3_op"].ap()[Dc + 2])],
                                      wait=op_mm[Dc], stream=f"b3op{Dc % 2}", ghost=True)
    bk.pop_scope()


def emit_block3_ffn(bk, cm, Wap):
    nc = bk.nc
    s = bk.seq
    x_fm, xn_fm = cm["x_fm"], cm["xn_fm"]
    psA, psB = cm["psA"], cm["psB"]
    ones = cm["ones"]
    f_a, f_b = cm["f2k_a"], cm["f2k_b"]

    bk.push_scope()
    h_grp = bk.sb("b3h", [P, 8 * T], BF16)
    wg_sl = [bk.sb(f"b3wg{i}", [P, DC * P], BF16) for i in range(2)]
    wu_sl = [bk.sb(f"b3wu{i}", [P, DC * P], BF16) for i in range(2)]
    wd_sl = [bk.sb(f"b3wd{i}", [P, 8 * P], BF16) for i in range(2)]
    gs_sb = bk.sb("b3gs", [P, 64], F32)
    us_sb = bk.sb("b3us", [P, 64], F32)
    ds_sb = bk.sb("b3ds", [P, DC], F32)
    dsm_sb = bk.sb("b3dsm", [P, DC], F32)
    rs_sb = bk.sb("b3rs", [P, T], BF16)
    sxn_row = bk.sb("b3sxn", [1, T], BF16)
    n128 = bk.sb("b3n128", [1, P], BF16)

    s.dma("gpsimd", [
        (gs_sb[:], Wap["l3_gs"].ap()),
        (us_sb[:], Wap["l3_us"].ap()),
        (ds_sb[:], Wap["l3_ds"].ap()),
    ])
    s.step("vector", lambda v: v.tensor_scalar_mul(dsm_sb[:], ds_sb[:], -128.0))
    s.step("vector", lambda v: v.memset(n128[:], -128.0))
    def sxmm(t):
        r = None
        ones = cm["ones"]
        for tt in range(4):
            for kd in range(DC):
                r = t.matmul(psB[0:1, tt * 512:(tt + 1) * 512], ones[:, 0:1],
                             xn_fm[:, kd * T + tt * 512: kd * T + (tt + 1) * 512],
                             start=(kd == 0), stop=(kd == DC - 1))
        return r
    s.step("tensor", sxmm)
    s.step("vector", lambda v: v.tensor_copy(sxn_row[:], psB[0:1, 0:T]))

    td3 = lambda: {"tensor": s.cnt["tensor"]}
    vd3 = lambda: {"vector": s.cnt["vector"]}
    sd3 = lambda: {"scalar": s.cnt["scalar"]}
    dmm_last = None
    for g in range(8):
        ld_end = {}
        base3 = s.mark()
        for j in range(2):
            ic = g * 8 + j
            ld_end[j] = s.dma("gpsimd", [
                (wg_sl[j % 2][:], Wap["l3_gw"].ap()[ic]),
                (wu_sl[j % 2][:], Wap["l3_uw"].ap()[ic]),
            ], stream=f"b3gu{j % 2}", ghost=True, wait=base3)
        mmu_end = {}
        silu_end = {}
        ts_end = {}
        mul_end = {}
        for j in range(8):
            ic = g * 8 + j
            def mmg(t, j=j):
                r = None
                wsl = wg_sl[j % 2]
                for tt in range(4):
                    for kd in range(DC):
                        t.matmul(psA[:, tt * 512:(tt + 1) * 512],
                                 wsl[:, kd * P:(kd + 1) * P],
                                 xn_fm[:, kd * T + tt * 512: kd * T + (tt + 1) * 512],
                                 start=(kd == 0), stop=False)
                    r = t.matmul(psA[:, tt * 512:(tt + 1) * 512], n128[:],
                                 sxn_row[0:1, tt * 512:(tt + 1) * 512],
                                 start=False, stop=True)
                return r
            s.step("tensor", mmg, wait=Seq.merge(base3, ld_end[j],
                                                 silu_end.get(j - 1, None)))
            mg = td3()
            def mmu(t, j=j):
                r = None
                wsl = wu_sl[j % 2]
                for tt in range(4):
                    for kd in range(DC):
                        t.matmul(psB[:, tt * 512:(tt + 1) * 512],
                                 wsl[:, kd * P:(kd + 1) * P],
                                 xn_fm[:, kd * T + tt * 512: kd * T + (tt + 1) * 512],
                                 start=(kd == 0), stop=False)
                    r = t.matmul(psB[:, tt * 512:(tt + 1) * 512], n128[:],
                                 sxn_row[0:1, tt * 512:(tt + 1) * 512],
                                 start=False, stop=True)
                return r
            s.step("tensor", mmu, wait=ts_end.get(j - 1))
            mmu_end[j] = td3()
            s.step("scalar", lambda sc, ic=ic: sc.activation(
                f_a[:], psA[:], AF.Silu, scale=gs_sb[:, ic:ic + 1]),
                wait=Seq.merge(mg, mul_end.get(j - 1)))
            silu_end[j] = sd3()
            s.step("vector", lambda v, ic=ic: v.tensor_scalar(
                f_b[:], psB[:], us_sb[:, ic:ic + 1], None, op0=ALU.mult),
                wait=Seq.merge(mmu_end[j], mul_end.get(j - 1)))
            ts_end[j] = vd3()
            s.step("vector", lambda v, j=j: v.tensor_mul(
                h_grp[:, j * T:(j + 1) * T], f_a[:], f_b[:]),
                wait=Seq.merge(ts_end[j], silu_end[j]))
            mul_end[j] = vd3()
            if j + 2 < 8:
                ld_end[j + 2] = s.dma("gpsimd", [
                    (wg_sl[j % 2][:], Wap["l3_gw"].ap()[ic + 2]),
                    (wu_sl[j % 2][:], Wap["l3_uw"].ap()[ic + 2]),
                ], wait=mmu_end[j], stream=f"b3gu{j % 2}", ghost=True)
        h_done = Seq.merge(mul_end[6], mul_end[7])
        def rsmm(t):
            r = None
            for tt in range(4):
                for j in range(8):
                    r = t.matmul(psA[:, tt * 512:(tt + 1) * 512], ones[:],
                                 h_grp[:, j * T + tt * 512: j * T + (tt + 1) * 512],
                                 start=(j == 0), stop=(j == 7))
            return r
        s.step("tensor", rsmm, wait=h_done)
        rs_mark = td3()
        if g == 0:
            s.step("vector", lambda v: v.tensor_copy(rs_sb[:], psA[:]), wait=rs_mark)
        else:
            s.step("vector", lambda v: v.tensor_add(rs_sb[:], rs_sb[:], psA[:]), wait=rs_mark)
        rs_ev = vd3()
        dld = {}
        for j2 in range(2):
            dld[j2] = s.dma("gpsimd", [(wd_sl[j2][:],
                            Wap["l3_dw"].ap()[j2][:, g * 1024:(g + 1) * 1024])],
                            wait=Seq.merge(rs_mark, dmm_last),
                            stream=f"b3wd{j2}", ghost=True)
        dmm_end = {}
        ts2_end = {}
        for Hc in range(DC):
            ps = psA if Hc % 2 == 0 else psB
            def dmm(t, Hc=Hc, ps=ps):
                r = None
                for tt in range(4):
                    for j in range(8):
                        r = t.matmul(ps[:, tt * 512:(tt + 1) * 512],
                                     wd_sl[Hc % 2][:, j * P:(j + 1) * P],
                                     h_grp[:, j * T + tt * 512: j * T + (tt + 1) * 512],
                                     start=(j == 0), stop=(j == 7))
                return r
            s.step("tensor", dmm, wait=Seq.merge(
                dld[Hc], rs_ev if Hc < 2 else None, ts2_end.get(Hc - 2)))
            dmm_end[Hc] = td3()
            s.step("vector", lambda v, Hc=Hc, ps=ps: v.tensor_scalar(
                f_a[:] if Hc % 2 == 0 else f_b[:], ps[:],
                ds_sb[:, Hc:Hc + 1], None, op0=ALU.mult), wait=dmm_end[Hc])
            ts2_end[Hc] = vd3()
            def accx(v, Hc=Hc):
                xo = x_fm[:, Hc * T:(Hc + 1) * T]
                return v.tensor_add(xo, xo, f_a[:] if Hc % 2 == 0 else f_b[:])
            s.step("vector", accx, wait=ts2_end[Hc])
            if Hc + 2 < DC:
                dld[Hc + 2] = s.dma("gpsimd", [(wd_sl[Hc % 2][:],
                                    Wap["l3_dw"].ap()[Hc + 2][:, g * 1024:(g + 1) * 1024])],
                                    wait=dmm_end[Hc], stream=f"b3wd{Hc % 2}", ghost=True)
        dmm_last = dmm_end[DC - 1]
    for Hc in range(DC):
        def accc(v, Hc=Hc):
            xo = x_fm[:, Hc * T:(Hc + 1) * T]
            return v.scalar_tensor_tensor(xo, rs_sb[:], dsm_sb[:, Hc:Hc + 1], xo,
                                          op0=ALU.mult, op1=ALU.add)
        s.step("vector", accc)
    bk.pop_scope()


def prep_l2(inp):
    w = {}
    WqT = np.ascontiguousarray(np.asarray(inp["L2_attn_Wq"], dtype=np.float32).T)
    w["l2_wq"] = tile_k_major(WqT, P).astype(bfloat16)
    WkT = np.ascontiguousarray(np.asarray(inp["L2_attn_Wk"], dtype=np.float32).T)
    w["l2_wk"] = tile_k_major(WkT, P).astype(bfloat16)
    WvT = np.ascontiguousarray(np.asarray(inp["L2_attn_Wv"], dtype=np.float32).T)
    w["l2_wv"] = tile_k_major(WvT, P).astype(bfloat16)
    w["l2_bq"] = np.ascontiguousarray(
        np.asarray(inp["L2_attn_bq"], dtype=np.float32).reshape(16, P).T)
    w["l2_bk"] = np.asarray(inp["L2_attn_bk"], dtype=np.float32).reshape(1, 512).copy()
    w["l2_bv"] = np.asarray(inp["L2_attn_bv"], dtype=np.float32).reshape(1, 512).copy()
    WoT = np.ascontiguousarray(np.asarray(inp["L2_attn_Wo"], dtype=np.float32).T)
    w["l2_wo"] = np.ascontiguousarray(
        WoT.reshape(8, 2, P, DC, P).transpose(0, 2, 3, 1, 4).reshape(8, P, DC * 2 * P)).astype(bfloat16)
    Wg1T = np.ascontiguousarray(np.asarray(inp["L2_ffn_Wg1"], dtype=np.float32).T)
    w["l2_wg1"] = tile_k_major(Wg1T, 8)[0]
    w["l2_wg2"] = np.ascontiguousarray(np.asarray(inp["L2_ffn_Wg2"], dtype=np.float32).T)
    Wup = np.asarray(inp["L2_ffn_Wup"], dtype=np.float32)
    w["l2_wup"] = np.concatenate([tile_k_major(np.ascontiguousarray(Wup[e].T), P)
                                  for e in range(8)], axis=0).astype(bfloat16)
    w["l2_bup"] = np.ascontiguousarray(
        np.asarray(inp["L2_ffn_bup"], dtype=np.float32).reshape(8, 8, P).transpose(2, 0, 1).reshape(P, 64))
    Wdn = np.asarray(inp["L2_ffn_Wdown"], dtype=np.float32)
    w["l2_wdn"] = np.concatenate([tile_k_major(np.ascontiguousarray(Wdn[e].T), P)
                                  for e in range(8)], axis=0).astype(bfloat16)
    w["l2_bdn"] = np.ascontiguousarray(
        np.asarray(inp["L2_ffn_bdown"], dtype=np.float32).reshape(8, DC, P).transpose(2, 0, 1).reshape(P, 8 * DC))
    return w


def emit_block2(bk, cm, Wap):
    nc = bk.nc
    s = bk.seq
    x_fm, xn_fm = cm["x_fm"], cm["xn_fm"]
    psA, psB = cm["psA"], cm["psB"]
    ones, ident = cm["ones"], cm["ident"]
    identf = cm["ident_f32"]
    onesf = cm["onesf_row"]
    scr2k = cm["sq8k"]
    f_a, f_b = cm["f2k_a"], cm["f2k_b"]
    epsc = cm["epsc"]

    big = bk.sb("b2big", [P, 8192], BF16)
    wsl_q = bk.sb("b2wq", [P, DC * P], BF16)
    wsl_k = bk.sb("b2wk", [P, DC * P], BF16)
    wsl_v = bk.sb("b2wv", [P, DC * P], BF16)
    wo_sl = bk.sb("b2wo", [P, DC * 2 * P], BF16)
    wg1_sl = bk.sb("b2wg1", [P, P], BF16)
    wg2_sb = bk.sb("b2wg2", [8, 8], BF16)
    wup_sl = bk.sb("b2wup", [P, DC * P], BF16)
    wdn_sl = bk.sb("b2wdn", [P, 8 * P], BF16)
    bq_sb = bk.sb("b2bq", [P, 16], F32)
    bk_sb = bk.sb("b2bk", [1, 512], BF16)
    bv_sb = bk.sb("b2bv", [1, 512], BF16)
    bup_sb = bk.sb("b2bup", [P, 64], F32)
    bdn_sb = bk.sb("b2bdn", [P, 8 * DC], F32)
    q_h = bk.sb("b2qh", [P, T], BF16)
    sk_tok = bk.sb("b2sk", [P, DC * P], BF16)
    v_tok = bk.sb("b2vt", [P, DC * P], BF16)
    kv_sb = bk.sb("b2kv", [P, P], BF16)
    ksum_sb = bk.sb("b2ks", [P, 1], BF16)
    scT = bk.sb("b2sct", [8, T], BF16)
    mx8 = bk.sb("b2mx", [P, 8], F32)
    wcol = bk.sb("b2wcol", [P, 4], F32)
    big_f = big[:].bitcast(F32)
    t1T = big[0:8, 0:T]
    stok = big_f[:, 1024:1152]
    eqm = big_f[:, 1152:1280]

    s.dma("gpsimd", [
        (wg1_sl[:], Wap["l2_wg1"].ap()),
        (wg2_sb[:], Wap["l2_wg2"].ap()),
        (bq_sb[:], Wap["l2_bq"].ap()),
        (bk_sb[:], Wap["l2_bk"].ap()),
        (bv_sb[:], Wap["l2_bv"].ap()),
        (bup_sb[:], Wap["l2_bup"].ap()),
        (bdn_sb[:], Wap["l2_bdn"].ap()),
    ])

    emit_ln(bk, cm)

    SKIP_ATTN = _os.environ.get("B2_SKIP_ATTN", "") == "1"
    SKIP_FFN = _os.environ.get("B2_SKIP_FFN", "") == "1"

    def elu1(src_ps, dst, n):
        s.step("vector", lambda v: v.tensor_scalar_min(f_a[:, 0:n], src_ps, 0.0))
        s.step("scalar", lambda sc: sc.activation(f_a[:, 0:n], f_a[:, 0:n], AF.Exp))
        s.step("vector", lambda v: v.tensor_scalar_max(f_b[:, 0:n], src_ps, 0.0))
        s.step("vector", lambda v: v.tensor_add(dst, f_a[:, 0:n], f_b[:, 0:n]))

    for g in ([] if SKIP_ATTN else range(4)):
        s.dma("gpsimd", [
            (wsl_k[:], Wap["l2_wk"].ap()[g]),
            (wsl_v[:], Wap["l2_wv"].ap()[g]),
        ])
        for ti in range(DC):
            def kvproj(t, ti=ti):
                r = None
                for which, wsl, boff in ((0, wsl_k, 0), (1, wsl_v, 0)):
                    dstp = psB[:, which * 512: which * 512 + P]
                    for kd in range(DC):
                        t.matmul(dstp, xn_fm[:, kd * T + ti * P: kd * T + (ti + 1) * P],
                                 wsl[:, kd * P:(kd + 1) * P],
                                 start=(kd == 0), stop=False)
                    brow = (bk_sb if which == 0 else bv_sb)[0:1, g * P:(g + 1) * P]
                    r = t.matmul(dstp, ones[0:1, :], brow, start=False, stop=True)
                return r
            s.step("tensor", kvproj)
            elu1(psB[:, 0:P], sk_tok[:, ti * P:(ti + 1) * P], P)
            s.step("scalar", lambda sc, ti=ti: sc.activation(
                v_tok[:, ti * P:(ti + 1) * P], psB[:, 512:512 + P], AF.Copy))
        def kvmm(t):
            r = None
            for ti in range(DC):
                r = t.matmul(psA[:, 0:P], sk_tok[:, ti * P:(ti + 1) * P],
                             v_tok[:, ti * P:(ti + 1) * P],
                             start=(ti == 0), stop=(ti == DC - 1))
            for ti in range(DC):
                r = t.matmul(psA[:, 512:513], sk_tok[:, ti * P:(ti + 1) * P],
                             ones[:, 0:1], start=(ti == 0), stop=(ti == DC - 1))
            return r
        s.step("tensor", kvmm)
        s.step("vector", lambda v: v.tensor_copy(kv_sb[:], psA[:, 0:P]))
        s.step("vector", lambda v: v.tensor_copy(ksum_sb[:], psA[:, 512:513]))

        for h2 in range(4):
            h = g * 4 + h2
            s.dma("gpsimd", [(wsl_q[:], Wap["l2_wq"].ap()[h])])
            def qproj(t):
                r = None
                for tt in range(4):
                    for kd in range(DC):
                        r = t.matmul(psA[:, tt * 512:(tt + 1) * 512],
                                     wsl_q[:, kd * P:(kd + 1) * P],
                                     xn_fm[:, kd * T + tt * 512: kd * T + (tt + 1) * 512],
                                     start=(kd == 0), stop=(kd == DC - 1))
                return r
            s.step("tensor", qproj)
            s.step("scalar", lambda sc, h=h: sc.activation(
                f_a[:], psA[:], AF.Identity, bias=bq_sb[:, h:h + 1]))
            s.step("vector", lambda v: v.tensor_scalar_min(f_b[:], f_a[:], 0.0))
            s.step("scalar", lambda sc: sc.activation(f_b[:], f_b[:], AF.Exp))
            s.step("vector", lambda v: v.tensor_scalar_max(q_h[:], f_a[:], 0.0))
            s.step("vector", lambda v: v.tensor_add(q_h[:], q_h[:], f_b[:]))
            def qkvmm(t):
                r = None
                for tt in range(4):
                    r = t.matmul(psA[:, tt * 512:(tt + 1) * 512], kv_sb[:],
                                 q_h[:, tt * 512:(tt + 1) * 512], start=True, stop=True)
                for tt in range(4):
                    r = t.matmul(psB[0:1, tt * 512:(tt + 1) * 512], ksum_sb[:],
                                 q_h[:, tt * 512:(tt + 1) * 512], start=True, stop=True)
                return r
            s.step("tensor", qkvmm)
            s.step("vector", lambda v: v.tensor_copy(f_b[0:1, 0:T], psB[0:1, 0:T]))
            s.step("vector", lambda v: v.tensor_scalar(
                f_b[0:1, 0:T], f_b[0:1, 0:T], 1e-6, None, op0=ALU.add))
            s.step("vector", lambda v: v.reciprocal(f_b[0:1, 0:T], f_b[0:1, 0:T]))
            def nbc(t):
                r = None
                for tt in range(4):
                    r = t.matmul(psB[:, tt * 512:(tt + 1) * 512], ones[0:1, :],
                                 f_b[0:1, tt * 512:(tt + 1) * 512],
                                 start=True, stop=True)
                return r
            s.step("tensor", nbc)
            s.step("vector", lambda v: v.tensor_copy(f_a[:], psB[:]))
            def att_mul(v, h=h):
                dst = big[:, (h % 2) * T:(h % 2 + 1) * T]
                return v.tensor_mul(dst, psA[:], f_a[:])
            s.step("vector", att_mul)

            if h % 2 == 1:
                og = h // 2
                s.dma("gpsimd", [(wo_sl[:], Wap["l2_wo"].ap()[og])])
                for Dc in range(DC):
                    def omm(t, Dc=Dc):
                        r = None
                        for tt in range(4):
                            for fc in range(2):
                                r = t.matmul(psB[:, tt * 512:(tt + 1) * 512],
                                             wo_sl[:, (Dc * 2 + fc) * P:(Dc * 2 + fc + 1) * P],
                                             big[:, fc * T + tt * 512: fc * T + (tt + 1) * 512],
                                             start=(fc == 0), stop=(fc == 1))
                        return r
                    s.step("tensor", omm)

                    def accg(v, Dc=Dc):
                        xo = x_fm[:, Dc * T:(Dc + 1) * T]
                        return v.tensor_add(xo, xo, psB[:])
                    s.step("vector", accg)

    if SKIP_FFN:
        return

    def g1mm(t):
        r = None
        for tt in range(4):
            for kd in range(DC):
                r = t.matmul(psA[0:8, tt * 512:(tt + 1) * 512],
                             wg1_sl[:, kd * 8:(kd + 1) * 8],
                             xn_fm[:, kd * T + tt * 512: kd * T + (tt + 1) * 512],
                             start=(kd == 0), stop=(kd == DC - 1))
        return r
    s.step("tensor", g1mm)
    s.step("scalar", lambda sc: sc.activation(t1T, psA[0:8, 0:T], AF.Tanh))
    def g2mm(t):
        r = None
        for tt in range(4):
            r = t.matmul(psB[0:8, tt * 512:(tt + 1) * 512], wg2_sb[:],
                         t1T[0:8, tt * 512:(tt + 1) * 512], start=True, stop=True)
        return r
    s.step("tensor", g2mm)
    s.step("vector", lambda v: v.tensor_copy(f_a[0:8, 0:T], psB[0:8, 0:T]))
    psA_bf = psA[:].bitcast(BF16)
    def ltr(t):
        r = None
        for i in range(16):
            r = t.transpose(psA_bf[:, i * 8:(i + 1) * 8],
                            f_a[0:8, i * P:(i + 1) * P], ident[0:8, 0:8])
        return r
    s.step("tensor", ltr)
    s.step("vector", lambda v: v.tensor_copy(stok, psA_bf[:, 0:128]))
    for i in range(16):
        s.step("vector", lambda v, i=i: v.max(out=mx8[:], in_=stok[:, i * 8:(i + 1) * 8]))
        s.step("vector", lambda v: v.tensor_sub(wcol[:, 0:1], mx8[:, 0:1], mx8[:, 1:2]))
        s.step("scalar", lambda sc: sc.activation(wcol[:, 1:2], wcol[:, 0:1], AF.Sigmoid))
        s.step("vector", lambda v: v.tensor_scalar(
            wcol[:, 2:3], wcol[:, 1:2], -1.0, None, op0=ALU.mult))
        s.step("vector", lambda v: v.tensor_scalar(
            wcol[:, 2:3], wcol[:, 2:3], 1.0, None, op0=ALU.add))
        s.step("vector", lambda v, i=i: v.tensor_scalar(
            eqm[:, i * 8:(i + 1) * 8], stok[:, i * 8:(i + 1) * 8], mx8[:, 0:1], None,
            op0=ALU.is_equal))
        s.step("vector", lambda v, i=i: v.tensor_scalar(
            eqm[:, i * 8:(i + 1) * 8], eqm[:, i * 8:(i + 1) * 8], wcol[:, 1:2], None,
            op0=ALU.mult))
        s.step("vector", lambda v, i=i: v.tensor_scalar(
            f_b[:, 0:8], stok[:, i * 8:(i + 1) * 8], mx8[:, 1:2], None, op0=ALU.is_equal))
        s.step("vector", lambda v, i=i: v.tensor_scalar(
            f_b[:, 0:8], f_b[:, 0:8], wcol[:, 2:3], None, op0=ALU.mult))
        s.step("vector", lambda v, i=i: v.tensor_add(
            eqm[:, i * 8:(i + 1) * 8], eqm[:, i * 8:(i + 1) * 8], f_b[:, 0:8]))
    def str2(t):
        r = None
        for i in range(16):
            r = t.transpose(psB[0:8, i * P:(i + 1) * P], eqm[:, i * 8:(i + 1) * 8],
                            identf[:])
        return r
    s.step("tensor", str2)
    s.step("vector", lambda v: v.tensor_copy(scT[:], psB[0:8, 0:T]))

    for e in range(8):
        def srow(t, e=e):
            r = None
            for tt in range(4):
                r = t.matmul(psB[0:1, tt * 512:(tt + 1) * 512], ident[0:8, e:e + 1],
                             scT[:, tt * 512:(tt + 1) * 512], start=True, stop=True)
            return r
        s.step("tensor", srow)
        s.step("vector", lambda v: v.tensor_copy(f_a[0:1, 0:T], psB[0:1, 0:T]))
        def sbc(t):
            r = None
            for tt in range(4):
                r = t.matmul(psB[:, tt * 512:(tt + 1) * 512], ones[0:1, :],
                             f_a[0:1, tt * 512:(tt + 1) * 512], start=True, stop=True)
            return r
        s.step("tensor", sbc)
        s.step("vector", lambda v: v.tensor_copy(f_b[:], psB[:]))

        upbuf = [wup_sl[:], wo_sl[:, 0:2048]]
        dnbuf = [wo_sl[:, 2048:3072], wo_sl[:, 3072:4096]]
        for half in range(2):
            t0 = half * 1024
            up_ld, up_mm, up_ev = {}, {}, {}
            base = s.mark()
            for ic in range(2):
                up_ld[ic] = s.dma("gpsimd", [(upbuf[ic % 2], Wap["l2_wup"].ap()[e * 8 + ic])],
                                  stream=f"b2up{ic % 2}", ghost=True, wait=base)
            for ic in range(8):
                po = (ic % 2) * 1024
                def umm(t, ic=ic, t0=t0, po=po):
                    r = None
                    for tt in range(2):
                        for kd in range(DC):
                            r = t.matmul(psA[:, po + tt * 512: po + (tt + 1) * 512],
                                         upbuf[ic % 2][:, kd * P:(kd + 1) * P],
                                         xn_fm[:, kd * T + t0 + tt * 512: kd * T + t0 + (tt + 1) * 512],
                                         start=(kd == 0), stop=(kd == DC - 1))
                    return r
                s.step("tensor", umm, wait=Seq.merge(base, up_ld[ic], up_ev.get(ic - 2)))
                up_mm[ic] = s.mark()
                s.step("scalar", lambda sc, e=e, ic=ic, po=po: sc.activation(
                    big[:, ic * 1024:(ic + 1) * 1024], psA[:, po:po + 1024], AF.Silu,
                    bias=bup_sb[:, e * 8 + ic: e * 8 + ic + 1]), wait=up_mm[ic])
                up_ev[ic] = s.mark()
                if ic + 2 < 8:
                    up_ld[ic + 2] = s.dma(
                        "gpsimd", [(upbuf[ic % 2], Wap["l2_wup"].ap()[e * 8 + ic + 2])],
                        stream=f"b2up{ic % 2}", ghost=True, wait=up_mm[ic])
            dn_ld, dn_mm, dn_ev, dn_ac = {}, {}, {}, {}
            base2 = s.mark()
            for Hc in range(2):
                dn_ld[Hc] = s.dma("gpsimd", [(dnbuf[Hc % 2], Wap["l2_wdn"].ap()[e * DC + Hc])],
                                  stream=f"b2dn{Hc % 2}", ghost=True, wait=base2)
            for Hc in range(DC):
                po = (Hc % 2) * 1024
                def dmm(t, Hc=Hc, po=po):
                    r = None
                    for tt in range(2):
                        for ic in range(8):
                            r = t.matmul(psA[:, po + tt * 512: po + (tt + 1) * 512],
                                         dnbuf[Hc % 2][:, ic * P:(ic + 1) * P],
                                         big[:, ic * 1024 + tt * 512: ic * 1024 + (tt + 1) * 512],
                                         start=(ic == 0), stop=(ic == 7))
                    return r
                s.step("tensor", dmm, wait=Seq.merge(base2, dn_ld[Hc], dn_ev.get(Hc - 2)))
                dn_mm[Hc] = s.mark()
                s.step("scalar", lambda sc, e=e, Hc=Hc, po=po: sc.activation(
                    f_a[:, po:po + 1024], psA[:, po:po + 1024], AF.Identity,
                    bias=bdn_sb[:, e * DC + Hc: e * DC + Hc + 1]),
                    wait=Seq.merge(dn_mm[Hc], dn_ac.get(Hc - 2)))
                dn_ev[Hc] = s.mark()
                s.step("vector", lambda v, t0=t0, po=po: v.tensor_mul(
                    f_a[:, po:po + 1024], f_a[:, po:po + 1024], f_b[:, t0:t0 + 1024]),
                    wait=dn_ev[Hc])
                def accx(v, Hc=Hc, t0=t0, po=po):
                    xo = x_fm[:, Hc * T + t0: Hc * T + t0 + 1024]
                    return v.tensor_add(xo, xo, f_a[:, po:po + 1024])
                s.step("vector", accx)
                dn_ac[Hc] = s.mark()
                if Hc + 2 < DC:
                    dn_ld[Hc + 2] = s.dma(
                        "gpsimd", [(dnbuf[Hc % 2], Wap["l2_wdn"].ap()[e * DC + Hc + 2])],
                        stream=f"b2dn{Hc % 2}", ghost=True, wait=dn_mm[Hc])



def build_program(blocks, weights_meta, xin_f32, xout_f32):
    nc = bass.Bass("TRN2", target_bir_lowering=False)
    ctx = ExitStack()
    Wap = {}
    for name, (shape, dt) in weights_meta.items():
        Wap[name] = nc.dram_tensor(name, list(shape), dt, kind="ExternalInput")
    if xin_f32:
        xin = nc.dram_tensor("xin", [D, T], F32, kind="ExternalInput")
    else:
        xin = nc.dram_tensor("xin", [P, DC * T], BF16, kind="ExternalInput")
    if xout_f32:
        xout = nc.dram_tensor("xout", [D, T], F32, kind="ExternalOutput")
    else:
        xout = nc.dram_tensor("xout", [P, DC * T], BF16, kind="ExternalOutput")

    with ctx:
        ctx.enter_context(nc.allow_low_precision(
            reason="bf16 stream/intermediates by design; tolerance bar "
                   "calibrated for bf16 matmuls + f32 accumulation"))
        block = ctx.enter_context(nc.Block())
        bk = BK(nc, ctx, block)
        s = bk.seq
        cm = {}
        cm["x_fm"] = bk.sb("x_fm", [P, DC * T], BF16)
        cm["xn_fm"] = bk.sb("xn_fm", [P, DC * T], BF16)
        cm["ones"] = bk.sb("ones", [P, P], BF16)
        cm["ident"] = bk.sb("ident", [P, P], BF16)
        cm["ident_f32"] = bk.sb("ident_f32", [P, P], F32)
        cm["onesf_row"] = bk.sb("onesf_row", [1, P], F32)
        cm["sq8k"] = bk.sb("sq8k", [P, T], BF16)
        cm["f2k_a"] = bk.sb("f2k_a", [P, T], BF16)
        cm["f2k_b"] = bk.sb("f2k_b", [P, T], BF16)
        cm["epsc"] = bk.sb("epsc", [P, 2], F32)
        cm["onec"] = bk.sb("onec", [P, 1], F32)
        cm["psA"] = bk.ps("psA", [P, 2048], F32)
        cm["psB"] = bk.ps("psB", [P, 2048], F32)

        s.dma("gpsimd", [
            (cm["ones"][:], Wap["ones_bf"].ap()),
            (cm["ident"][:], Wap["ident_bf"].ap()),
        ], wait={})
        s.step("vector", lambda v: v.memset(cm["onesf_row"][:], 1.0), wait={})
        s.step("vector", lambda v: v.memset(cm["epsc"][:, 0:1], 1e-5), wait={})
        s.step("vector", lambda v: v.memset(cm["epsc"][:, 1:2], 1e-6), wait={})
        s.step("vector", lambda v: v.memset(cm["onec"][:], 1.0), wait={})
        s.step("vector", lambda v: v.tensor_copy(cm["ident_f32"][:], cm["ident"][:]))

        if xin_f32:
            s.dma("gpsimd", [
                (cm["x_fm"][:, c * 4 * T:(c + 1) * 4 * T].rearrange("p (c t) -> p c t", t=T),
                 xin.ap().rearrange("(c p) t -> p c t", p=P)[:, c * 4:(c + 1) * 4, :])
                for c in range(4)
            ], wait={})
        else:
            s.dma("sync", [(cm["x_fm"][:], xin.ap())], wait={})

        for bi in blocks:
            bk.push_scope()
            if bi == 0:
                emit_block0(bk, cm, Wap)
            elif bi == 1:
                emit_block1(bk, cm, Wap)
            elif bi == 2:
                emit_block2(bk, cm, Wap)
            elif bi == 3:
                emit_block3(bk, cm, Wap)
            else:
                raise NotImplementedError(f"block {bi}")
            bk.pop_scope()

        if xout_f32:
            s.dma("gpsimd", [
                (xout.ap().rearrange("(c p) t -> p c t", p=P)[:, c * 4:(c + 1) * 4, :],
                 cm["x_fm"][:, c * 4 * T:(c + 1) * 4 * T].rearrange("p (c t) -> p c t", t=T))
                for c in range(4)
            ])
        else:
            s.dma("gpsimd", [(xout.ap(), cm["x_fm"][:])])

        def fin(g):
            s._emit_waits(g, "gpsimd", s.mark())
        block.gpsimd(fin)
    return nc



def _prep_all(inputs, blocks):
    shared = {}
    shared.update(host_consts())
    if 0 in blocks:
        shared.update(prep_l0(inputs))
    if 1 in blocks:
        shared.update(prep_l1(inputs))
    if 2 in blocks:
        shared.update(prep_l2(inputs))
    if 3 in blocks:
        shared.update(prep_l3(inputs))
    return shared


def to_dev_layout(xr):
    return np.ascontiguousarray(
        xr.T.astype(bfloat16).reshape(DC, P, T).transpose(1, 0, 2).reshape(P, DC * T))


def from_dev_layout(a):
    return np.ascontiguousarray(
        np.asarray(a).reshape(P, DC, T).transpose(1, 0, 2).reshape(D, T).T
    ).astype(np.float32)


def kernel(**inputs):
    x = np.asarray(inputs["L0_attn_x"], dtype=np.float32)
    blocks_env = _os.environ.get("K_BLOCKS", "")
    if blocks_env:
        launches = [[int(b) for b in blocks_env.split(",")]]
    elif _os.environ.get("K_UNFUSED", "") == "1":
        launches = [[0], [1], [2], [3]]
    else:
        launches = [[0, 1, 2, 3]]

    state = None
    out = None
    for li, blocks in enumerate(launches):
        first = li == 0
        last = li == len(launches) - 1
        shared = _prep_all(inputs, blocks)
        weights_meta = {k: (v.shape, BF16 if v.dtype == bfloat16 else
                            (I32 if v.dtype == np.int32 else F32))
                        for k, v in shared.items()}
        weights_meta["rank_sel"] = ((P, P), F32)
        nc = build_program(blocks, weights_meta, xin_f32=first, xout_f32=last)
        in_maps = []
        for r in range(NCORES):
            m = dict(shared)
            rs = np.zeros((P, P), dtype=np.float32)
            rs[16 * r:16 * (r + 1), :] = 1.0
            m["rank_sel"] = rs
            m["xin"] = (np.ascontiguousarray(x[r].T) if first else state[r])
            in_maps.append(m)
        res = run_bass_kernel_spmd(nc, in_maps, core_ids=list(range(NCORES)))
        if last:
            out = np.empty((NCORES, T, D), dtype=np.float32)
            for r in range(NCORES):
                out[r] = np.asarray(res.results[r]["xout"], dtype=np.float32).T
        else:
            state = [np.asarray(res.results[r]["xout"]) for r in range(NCORES)]
    return out
